# version 02
# baseline (speedup 1.0000x reference)
import numpy as np
import concourse.bass as bass
import concourse.mybir as mybir
from concourse.bass_utils import run_bass_kernel_spmd

F32 = mybir.dt.float32
BF16 = mybir.dt.bfloat16
I32 = mybir.dt.int32
AF = mybir.ActivationFunctionType
ALU = mybir.AluOpType
AX = mybir.AxisListType

NCORE = 8
D = 2048
T = 2048
NT = T // 128
NKC = D // 128
EPS_LN = 1e-5
HID_E = [1024, 2048, 3072, 4096, 1024, 2048, 3072, 4096]
COST_LAMBDA = 1e-7
LDIM = 1024



class Prog:
    ENGINES = ("sync", "tensor", "vector", "scalar", "gpsimd")

    def __init__(self, nc):
        self.nc = nc
        self.stream = {e: [] for e in self.ENGINES}
        self._sems = {}
        self._stg = 0

    def sem(self, name):
        if name not in self._sems:
            self._sems[name] = [self.nc.alloc_semaphore(name), 0]
        return self._sems[name][0]

    def val(self, name):
        self.sem(name)
        return self._sems[name][1]

    def bump(self, name, n):
        self.sem(name)
        self._sems[name][1] += n
        return self._sems[name][1]

    def on(self, eng, fn):
        self.stream[eng].append(fn)

    def dma(self, eng, out, in_, sem, slow=False):
        tgt = self.bump(sem, 16)
        s = self.sem(sem)

        def f(e, out=out, in_=in_, s=s, slow=slow):
            e.dma_start(out=out, in_=in_, allow_slow_non_contiguous=slow).then_inc(s, 16)
        self.on(eng, f)
        return tgt

    def wait(self, eng, sem, val):
        s = self.sem(sem)
        self.on(eng, lambda e, s=s, v=val: e.wait_ge(s, v))

    def do(self, eng, fn, inc=None):
        if inc is None:
            self.on(eng, lambda e, fn=fn: fn(e))
            return None
        if isinstance(inc, tuple):
            tgts = [self.bump(n, 1) for n in inc]
            ss = [self.sem(n) for n in inc]

            def run2(e, fn=fn, ss=ss):
                i = fn(e)
                for s in ss:
                    i = i.then_inc(s, 1)
            self.on(eng, run2)
            return tgts[0]
        tgt = self.bump(inc, 1)
        s = self.sem(inc)
        self.on(eng, lambda e, fn=fn, s=s: fn(e).then_inc(s, 1))
        return tgt

    def drain(self, eng):
        self.on(eng, lambda e: e.drain())

    def mark(self, eng, sem):
        tgt = self.bump(sem, 1)
        s = self.sem(sem)
        self.on(eng, lambda e, s=s: e.sem_inc(s, 1))
        return tgt

    def barrier(self, engines=None):
        engines = engines or self.ENGINES
        k = self.bump("bar", len(engines))
        s = self.sem("bar")
        for ename in engines:
            def f(e, s=s, k=k):
                e.drain()
                e.sem_inc(s, 1)
                e.wait_ge(s, k)
            self.on(ename, f)

    def emit(self):
        with self.nc.Block() as block:
            for ename in self.ENGINES:
                fns = self.stream[ename]

                def run(eng, fns=fns):
                    for f in fns:
                        f(eng)
                getattr(block, ename)(run)


def _reshape(v, shape):
    if len(shape) > 2:
        names = " ".join(f"a{i}" for i in range(len(shape) - 1))
        v = v.rearrange(f"p ({names}) -> p {names}",
                        **{f"a{i}": shape[i + 1] for i in range(len(shape) - 1)})
    return v


class Arena:
    def __init__(self, nc, p):
        import ml_dtypes
        self.nc = nc
        self.ab = nc.alloc_sbuf_tensor("AR_BF", [128, 76800], BF16)
        self.asm = nc.alloc_sbuf_tensor("AR_SM", [128, 2048], F32)
        self.stg = nc.alloc_sbuf_tensor("AR_STG", [128, 4096], F32)
        self.rows = nc.alloc_sbuf_tensor("AR_ROWS", [1, 4096], F32)
        self.rowsbf = nc.alloc_sbuf_tensor("AR_ROWSBF", [1, 4096], BF16)
        self.ps = [nc.alloc_psum_tensor(f"PS{i}", [128, 512], F32) for i in range(6)]
        self.pt = [nc.alloc_psum_tensor(f"PT{i}", [128, 1024], BF16) for i in range(2)]
        self.ident = nc.alloc_sbuf_tensor("IDENT", [128, 128], BF16)
        self.identf = nc.alloc_sbuf_tensor("IDENTF", [128, 128], F32)
        self.ones1 = nc.alloc_sbuf_tensor("ONES1", [1, 128], BF16)
        self.negcol = nc.alloc_sbuf_tensor("NEGCOL", [128, 1], F32)
        ident_np = np.eye(128, dtype=np.float32).astype(ml_dtypes.bfloat16)
        self.ident_dram = nc.inline_tensor(ident_np, "identc").ap()
        self.identf_dram = nc.inline_tensor(np.eye(128, dtype=np.float32), "identfc").ap()
        t = p.dma("sync", self.ident[:, :], self.ident_dram[:, :], "ldc")
        t = p.dma("sync", self.identf[:, :], self.identf_dram[:, :], "ldc")
        p.wait("vector", "ldc", t)
        p.on("vector", lambda e: e.memset(self.negcol[:, :], -1.0))
        p.do("vector", lambda e: e.memset(self.ones1[:, :], 1.0), inc="vc")
        cr = p.val("vc")
        for eng in ("tensor", "scalar", "gpsimd", "sync"):
            p.wait(eng, "vc", cr)
        p.wait("tensor", "ldc", t)

    def bf(self, off, shape):
        n = int(np.prod(shape[1:]))
        return _reshape(self.ab[:, off:off + n], shape)

    def f32(self, off, shape):
        assert off % 2 == 0
        n = int(np.prod(shape[1:])) * 2
        return _reshape(self.ab[:, off:off + n].bitcast(F32), shape)

    def sm(self, off, shape):
        n = int(np.prod(shape[1:]))
        return _reshape(self.asm[:, off:off + n], shape)


def stream_cast(p, ar, src3, dst3, nkc, csz, tag="stg"):
    tgt = None
    kpp = max(1, 2048 // csz)
    k0 = 0
    while k0 < nkc:
        kk = min(kpp, nkc - k0)
        n = kk * csz
        slot = p._stg % 2
        p._stg += 1
        st = ar.stg[:, slot * 2048: slot * 2048 + n]
        if p._stg > 2:
            p.wait("sync", f"{tag}c", p.val(f"{tag}c") - 1)
        if nkc == 1:
            srcp, dstp = src3, dst3
        else:
            st = _reshape(st, [128, kk, csz])
            srcp = src3[:, k0:k0 + kk, :]
            dstp = dst3[:, k0:k0 + kk, :]
        ldt = p.dma("sync", st, srcp, f"{tag}l{slot}")
        p.wait("gpsimd", f"{tag}l{slot}", ldt)
        if p.val(f"{tag}c") >= 2:
            p.wait("gpsimd", f"{tag}c", p.val(f"{tag}c") - 1)
        tgt = p.do("gpsimd", lambda e, st=st, dstp=dstp: e.tensor_copy(dstp, st),
                   inc=f"{tag}c")
        k0 += kk
    return tgt


def wcols(w_dram, c0, c1):
    return w_dram[:, c0:c1].rearrange("(kc p) c -> p kc c", p=128)


def brow(b_dram):
    return b_dram[:].rearrange("(o a) -> o a", o=1)


def bcol(b_dram):
    return b_dram[:].rearrange("(c p) -> p c", p=128)



def emit_ln_fm(p, ar, x_dram, tok0, ntt, ln_fm, scr, tag):
    xa = [ar.f32(scr + i * 4096, [128, 2048]) for i in range(2)]
    junk = [ar.bf(scr + 8192 + i * 2048, [128, 2048]) for i in range(2)]
    lntm = [ar.bf(scr + 12288 + i * 2048, [128, 2048]) for i in range(2)]
    stats = ar.sm(0, [128, 80])
    ld, sv, vv, tv, ev = (f"{tag}ld", f"{tag}s", f"{tag}v", f"{tag}t", f"{tag}e")

    sv0, vv0 = p.val(sv), p.val(vv)
    for i in range(ntt):
        if i >= 2:
            p.wait("sync", sv, sv0 + (i - 1))
            p.wait("sync", vv, vv0 + (i - 1))
        ldp = f"{ld}{i % 2}"
        ldt = p.dma("sync", xa[i % 2][:, :],
                    x_dram[tok0 + i * 128: tok0 + (i + 1) * 128, :], ldp)
        p.wait("scalar", ldp, ldt)
        p.do("scalar", lambda e, i=i, xi=xa[i % 2]: e.activation(
            out=junk[i % 2][:, :], in_=xi[:, :], func=AF.Square,
            accum_out=stats[:, 16 + i:17 + i]), inc=sv)
        p.wait("vector", ldp, ldt)
        p.do("vector", lambda e, i=i, xi=xa[i % 2]: e.reduce_sum(
            stats[:, i:i + 1], xi[:, :], axis=AX.X), inc=vv)

    p.wait("vector", sv, p.val(sv))
    p.wait("vector", vv, p.val(vv))
    mu, var, nmu = stats[:, 32:48], stats[:, 48:64], stats[:, 64:80]

    def small(e, ntt=ntt):
        e.tensor_scalar(mu[:, 0:ntt], stats[:, 0:ntt], 1.0 / D, None, op0=ALU.mult)
        e.tensor_scalar(var[:, 0:ntt], stats[:, 16:16 + ntt], 1.0 / D, None,
                        op0=ALU.mult)
        e.drain()
        e.tensor_tensor(nmu[:, 0:ntt], mu[:, 0:ntt], mu[:, 0:ntt], op=ALU.mult)
        e.drain()
    p.on("vector", small)
    p.on("vector", lambda e: e.tensor_tensor(
        var[:, 0:ntt], var[:, 0:ntt], nmu[:, 0:ntt], op=ALU.subtract))
    p.drain("vector")
    vs = p.do("vector", lambda e: e.tensor_scalar(
        var[:, 0:ntt], var[:, 0:ntt], EPS_LN, None, op0=ALU.add), inc=vv)
    p.wait("scalar", vv, vs)
    ss = p.do("scalar", lambda e: e.activation(out=var[:, 0:ntt], in_=var[:, 0:ntt],
                                               func=AF.Sqrt), inc=sv)
    p.wait("vector", sv, ss)
    p.on("vector", lambda e: e.reciprocal(var[:, 0:ntt], var[:, 0:ntt]))
    p.drain("vector")
    vs2 = p.do("vector", lambda e: e.scalar_tensor_tensor(
        nmu[:, 0:ntt], mu[:, 0:ntt], -1.0, var[:, 0:ntt],
        op0=ALU.mult, op1=ALU.mult), inc=vv)
    p.wait("scalar", vv, vs2)

    sv1 = p.val(sv)
    vv1 = p.val(vv)
    evbase = p.val(ev)
    tvbase = p.val(tv)
    for i in range(ntt):
        if i < 2:
            p.wait("sync", sv, sv1)
            p.wait("sync", vv, vv1)
        if i >= 2:
            p.wait("sync", sv, sv1 + (i - 1))
        ldp = f"{ld}{i % 2}"
        ldt = p.dma("sync", xa[i % 2][:, :],
                    x_dram[tok0 + i * 128: tok0 + (i + 1) * 128, :], ldp)
        p.wait("scalar", ldp, ldt)
        if i >= 2:
            p.wait("scalar", tv, tvbase + (i - 2) * 2 + 2)
        slot = lntm[i % 2]
        p.do("scalar", lambda e, i=i, xi=xa[i % 2], slot=slot: e.activation(
            out=slot[:, :], in_=xi[:, :], func=AF.Identity,
            bias=nmu[:, i:i + 1], scale=var[:, i:i + 1]), inc=sv)
        svv = p.val(sv)
        p.wait("tensor", sv, svv)
        for half in range(2):
            bank = ar.pt[half]
            if i >= 1:
                p.wait("tensor", ev, evbase + (i - 1) * 2 + half + 1)
            for j in range(8):
                jj = half * 8 + j
                if j < 7:
                    p.do("tensor", lambda e, slot=slot, jj=jj, bank=bank, j=j:
                         e.transpose(bank[:, j * 128:(j + 1) * 128],
                                     slot[:, jj * 128:(jj + 1) * 128],
                                     ar.ident[:, :]))
                else:
                    tt = p.do("tensor", lambda e, slot=slot, jj=jj, bank=bank, j=j:
                              e.transpose(bank[:, j * 128:(j + 1) * 128],
                                          slot[:, jj * 128:(jj + 1) * 128],
                                          ar.ident[:, :]), inc=tv)
            p.wait("vector", tv, tt)
            dst = ln_fm[:, half * 8:(half + 1) * 8, i * 128:(i + 1) * 128]
            src = bank[:].rearrange("p (a b) -> p a b", a=8)
            p.do("vector", lambda e, dst=dst, src=src: e.tensor_copy(dst, src),
                 inc=ev)
    return p.val(ev)



def emit_moe(p, ar, io):
    nc = p.nc
    TH = 1024
    NTH = TH // 128
    ones = ar.ones1

    dbc = ar.sm(128, [128, 8])
    sb1c = ar.sm(136, [128, 8])
    b1c = ar.sm(160, [128, 32])
    nb2c = ar.sm(192, [128, 8])
    w8 = ar.sm(200, [128, 64])
    topm = ar.sm(264, [128, 56])
    ctile = ar.sm(320, [128, 80])
    ftmp = ar.sm(400, [128, 1024])
    cacc = ar.sm(1424, [128, 8])
    b2s = ar.sm(1432, [128, 8])
    t_db = p.dma("sync", dbc[:, :], bcol(io["down_b"]), "mbl", slow=True)
    t_sb1 = p.dma("sync", sb1c[:, :], bcol(io["shared_b1"]), "mbl", slow=True)

    cost = np.array([HID_E[i] * 2 * 1024 for i in range(8)], np.float32) * COST_LAMBDA
    cost_dram = nc.inline_tensor(cost.reshape(1, 8), "moe_cost").ap()
    t1 = p.dma("sync", ar.rows[0:1, 0:8], brow(io["router_b"]), "mbl")
    t2 = p.dma("sync", ar.rows[0:1, 8:16], cost_dram[:, :], "mbl")
    rb_bf = ar.rowsbf[0:1, 0:8]
    p.wait("vector", "mbl", p.val("mbl"))
    p.do("vector", lambda e: e.tensor_tensor(rb_bf, ar.rows[0:1, 0:8],
                                             ar.rows[0:1, 8:16], op=ALU.subtract),
         inc="mbv")
    p.wait("sync", "mbv", p.val("mbv"))
    t3 = p.dma("sync", ar.rows[0:1, 1024:3072], brow(io["up_b"]), "mbl")
    t4 = p.dma("sync", ar.rows[0:1, 0:1024],
               brow(io["core_b"])[0:1, 0:1024], "mbl")
    t4b = p.dma("sync", ar.rows[0:1, 3072:4096],
                brow(io["core_b"])[0:1, 1024:2048], "mbl")
    ubrow_bf = ar.rowsbf[0:1, 1024:3072]
    p.wait("vector", "mbl", p.val("mbl"))
    p.on("vector", lambda e: e.tensor_tensor(
        ubrow_bf[0:1, 0:1024], ar.rows[0:1, 1024:2048], ar.rows[0:1, 0:1024],
        op=ALU.add))
    p.do("vector", lambda e: e.tensor_tensor(
        ubrow_bf[0:1, 1024:2048], ar.rows[0:1, 2048:3072],
        ar.rows[0:1, 3072:4096], op=ALU.add), inc="mbv")
    p.wait("sync", "mbv", p.val("mbv"))
    t5 = p.dma("sync", ar.rows[0:1, 0:1024], brow(io["shared_b2"]), "mbl")
    sb2row_bf = ar.rowsbf[0:1, 3072:4096]
    p.wait("vector", "mbl", p.val("mbl"))
    p.do("vector", lambda e: e.tensor_copy(sb2row_bf, ar.rows[0:1, 0:1024]),
         inc="mbv")
    rows_ready = p.val("mbv")

    rw_bf = ar.bf(75776, [128, 16, 8])
    rw_t = stream_cast(p, ar, wcols(io["router_w"], 0, 8), rw_bf, 16, 8)

    LN0, GH0, R1, ACC0 = 0, 16384, 24576, 57344
    w10t = ar.bf(73728, [16, 1024])
    cm10 = ar.bf(74752, [16, 1024])

    for half in range(2):
        tok0 = half * TH
        hw = f"h{half}m"
        ln_fm = ar.bf(LN0, [128, 16, TH])
        gh_fm = ar.bf(GH0, [128, 8, TH])
        acc = ar.f32(ACC0, [128, 8, TH])

        emit_ln_fm(p, ar, io["xin"], tok0, NTH, ln_fm, R1, f"mL{half}")
        p.barrier()

        p.wait("tensor", "stgc", rw_t)
        p.wait("tensor", "mbv", rows_ready)
        for i in range(NTH):
            bank = (ar.ps[4] if i % 2 == 0 else ar.ps[5])[:, 0:8]
            if i >= 2:
                p.wait("tensor", f"{hw}rv", p.val(f"{hw}rv") - 1)
            for kc in range(NKC):
                p.do("tensor", lambda e, i=i, kc=kc, bank=bank: e.matmul(
                    bank, ln_fm[:, kc, i * 128:(i + 1) * 128], rw_bf[:, kc, :],
                    start=(kc == 0), stop=False))
            tt = p.do("tensor", lambda e, bank=bank: e.matmul(
                bank, ones[:, :], rb_bf, start=False, stop=True), inc=f"{hw}rt")
            p.wait("vector", f"{hw}rt", tt)
            lg, m1, ssum = topm[:, 0:8], topm[:, 8:9], topm[:, 9:10]

            def rpre(e, bank=bank, lg=lg, m1=m1):
                e.tensor_copy(lg, bank)
                e.drain()
                e.reduce_max(m1, lg, axis=AX.X)
                e.drain()
            p.on("vector", rpre)
            vt = p.do("vector", lambda e, m1=m1: e.tensor_scalar(
                topm[:, 10:11], m1, -1.0, None, op0=ALU.mult), inc=f"{hw}rv0")
            p.wait("scalar", f"{hw}rv0", vt)
            pr = topm[:, 16:24]
            st = p.do("scalar", lambda e, lg=lg, pr=pr, ssum=ssum: e.activation(
                out=pr, in_=lg, func=AF.Exp, bias=topm[:, 10:11], accum_out=ssum),
                inc=f"{hw}rs")
            p.wait("vector", f"{hw}rs", st)

            def top2(e, pr=pr, ssum=ssum):
                rs = topm[:, 11:12]
                e.reciprocal(rs, ssum)
                e.drain()
                e.tensor_scalar(pr, pr, rs, None, op0=ALU.mult)
                e.drain()
                e.reduce_max(topm[:, 24:25], pr, axis=AX.X)
                e.drain()
                e.tensor_scalar(topm[:, 32:40], pr, topm[:, 24:25], None,
                                op0=ALU.is_ge)
                e.drain()
                e.scalar_tensor_tensor(topm[:, 40:48], topm[:, 32:40], -10.0, pr,
                                       op0=ALU.mult, op1=ALU.add)
                e.drain()
                e.reduce_max(topm[:, 25:26], topm[:, 40:48], axis=AX.X)
                e.drain()
            p.on("vector", top2)
            vt2 = p.do("vector", lambda e: e.tensor_tensor(
                topm[:, 26:27], topm[:, 24:25], topm[:, 25:26], op=ALU.subtract),
                inc=f"{hw}rv1")
            p.wait("scalar", f"{hw}rv1", vt2)
            st2 = p.do("scalar", lambda e: e.activation(
                out=topm[:, 27:28], in_=topm[:, 26:27], func=AF.Sigmoid),
                inc=f"{hw}rs2")
            p.wait("vector", f"{hw}rs2", st2)

            def mkw(e, i=i):
                g1, g2 = topm[:, 27:28], topm[:, 28:29]
                e.tensor_scalar(g2, g1, -1.0, 1.0, op0=ALU.mult, op1=ALU.add)
                mk2 = topm[:, 48:56]
                e.tensor_scalar(mk2, topm[:, 40:48], topm[:, 25:26], None,
                                op0=ALU.is_ge)
                wc = w8[:, i * 8:(i + 1) * 8]
                e.tensor_scalar(wc, topm[:, 32:40], g1, None, op0=ALU.mult)
                e.drain()
            p.on("vector", mkw)
            p.do("vector", lambda e, i=i: e.scalar_tensor_tensor(
                w8[:, i * 8:(i + 1) * 8], topm[:, 48:56], topm[:, 28:29],
                w8[:, i * 8:(i + 1) * 8], op0=ALU.mult, op1=ALU.add),
                inc=f"{hw}rv")

        p.wait("vector", f"{hw}rv", p.val(f"{hw}rv"))
        p.do("vector", lambda e: e.memset(w10t[:, :], -1.0), inc=f"{hw}wv")
        p.wait("tensor", f"{hw}wv", p.val(f"{hw}wv"))
        for i in range(NTH):
            tt = p.do("tensor", lambda e, i=i: e.transpose(
                ar.ps[4][0:8, 0:128], w8[:, i * 8:(i + 1) * 8], ar.identf[:, :]),
                inc=f"{hw}wt")
            p.wait("vector", f"{hw}wt", tt)
            p.drain("vector")
            vv2 = p.do("vector", lambda e, i=i: e.tensor_copy(
                w10t[0:8, i * 128:(i + 1) * 128], ar.ps[4][0:8, 0:128]),
                inc=f"{hw}wv")
            p.wait("tensor", f"{hw}wv", vv2)

        p.wait("scalar", "mbl", p.val("mbl"))
        for hc in range(8):
            woff = R1 + (hc % 2) * 2048
            wsl = ar.bf(woff, [128, 16, 128])
            if hc >= 2:
                p.wait("gpsimd", f"{hw}ht", p.val(f"{hw}ht") - 2)
            wt = stream_cast(p, ar, wcols(io["down_w"], hc * 128, (hc + 1) * 128),
                             wsl, 16, 128)
            p.wait("tensor", "stgc", wt)
            for tf in range(2):
                bank = ar.ps[tf]
                if p.val(f"{hw}hs") >= 2:
                    p.wait("tensor", f"{hw}hs", p.val(f"{hw}hs") - 1)
                for kc in range(NKC):
                    tt = p.do("tensor", lambda e, kc=kc, tf=tf, bank=bank, wsl=wsl:
                              e.matmul(bank[:, :], wsl[:, kc, :],
                                       ln_fm[:, kc, tf * 512:(tf + 1) * 512],
                                       start=(kc == 0), stop=(kc == NKC - 1)),
                              inc=(f"{hw}ht" if kc == NKC - 1 else None))
                p.wait("scalar", f"{hw}ht", tt)
                p.do("scalar", lambda e, hc=hc, tf=tf, bank=bank: e.activation(
                    out=gh_fm[:, hc, tf * 512:(tf + 1) * 512], in_=bank[:, :],
                    func=AF.Gelu, bias=dbc[:, hc:hc + 1]), inc=f"{hw}hs")
        h_done = p.val(f"{hw}hs")

        p.wait("scalar", f"{hw}rt", p.val(f"{hw}rt"))
        p.wait("scalar", f"{hw}ht", p.val(f"{hw}ht"))
        for kc in range(NKC):
            p.do("scalar", lambda e, kc=kc: e.activation(
                out=ln_fm[:, kc, :], in_=ln_fm[:, kc, :], func=AF.Gelu),
                inc=f"{hw}gs")
        gln_done = p.val(f"{hw}gs")
        p.wait("tensor", f"{hw}gs", gln_done)
        ncst = 0
        for oR in range(4):
            woff = R1 + 20480 + (oR % 2) * 8192
            if oR >= 2:
                p.wait("gpsimd", f"{hw}ct", p.val(f"{hw}ct") - 8)
            wsl = ar.bf(woff, [128, 16, 512])
            wt = stream_cast(p, ar, wcols(io["core_w"], oR * 512, (oR + 1) * 512),
                             wsl, 16, 512)
            p.wait("tensor", "stgc", wt)
            for i in range(NTH):
                bank = ar.ps[2 + i % 2]
                if p.val(f"{hw}cv") >= 2:
                    p.wait("tensor", f"{hw}cv", p.val(f"{hw}cv") - 1)
                for kc in range(NKC):
                    tt = p.do("tensor", lambda e, kc=kc, i=i, bank=bank, wsl=wsl:
                              e.matmul(bank[:, :],
                                       ln_fm[:, kc, i * 128:(i + 1) * 128],
                                       wsl[:, kc, :], start=(kc == 0),
                                       stop=(kc == NKC - 1)),
                              inc=(f"{hw}ct" if kc == NKC - 1 else None))
                p.wait("vector", f"{hw}ct", tt)
                ncst += 1
                cpar = f"{hw}cst{ncst % 2}"
                if ncst > 2:
                    p.wait("vector", cpar, p.val(cpar))
                vt = p.do("vector", lambda e, i=i, bank=bank:
                          e.tensor_copy(ftmp[:, (i % 2) * 512:(i % 2) * 512 + 512],
                                        bank[:, :]), inc=f"{hw}cv")
                p.wait("gpsimd", f"{hw}cv", vt)
                p.dma("gpsimd", io["core_scratch"][
                    tok0 + i * 128: tok0 + (i + 1) * 128,
                    oR * 512:(oR + 1) * 512],
                    ftmp[:, (i % 2) * 512:(i % 2) * 512 + 512], cpar)

        gE1 = ar.bf(R1, [128, 32, 512])
        w1sl = [ar.bf(R1 + 16384 + k * 1024, [128, 8, 128]) for k in range(2)]
        w2sl = [ar.bf(R1 + 18432 + k * 512, [128, 512]) for k in range(2)]
        gb1 = ar.bf(R1 + 19456, [128, 32])

        p.wait("scalar", "mbl", p.val("mbl"))
        if half == 0:
            p.do("vector", lambda e: e.memset(cm10[:, :], 0.0), inc="cmv")

        for e_i in range(9):
            shared = (e_i == 8)
            h_e = LDIM if shared else HID_E[e_i]
            nj = h_e // 128
            w1d = io["shared_w1"] if shared else io[f"e{e_i}_w1"]
            w2d = io["shared_w2"] if shared else io[f"e{e_i}_w2"]
            do_c = (not shared) and half == 0
            if not shared:
                p.wait("sync", f"{hw}e1", p.val(f"{hw}e1"))
                p.wait("sync", f"{hw}ce", p.val(f"{hw}ce"))
                tb = p.dma("sync", b1c[:, 0:nj], bcol(io[f"e{e_i}_b1"]), "b1l",
                           slow=True)
                p.wait("scalar", "b1l", tb)
                p.wait("scalar", f"{hw}ftj", p.val(f"{hw}ftj"))
                p.wait("scalar", f"{hw}ft", p.val(f"{hw}ft"))
                gb_t = p.do("scalar", lambda e, nj=nj: e.activation(
                    out=gb1[:, 0:nj], in_=b1c[:, 0:nj], func=AF.Gelu),
                    inc=f"{hw}gb")
                if do_c:
                    p.wait("sync", f"{hw}nb", p.val(f"{hw}nb"))
                    tb2 = p.dma("sync", nb2c[:, :], bcol(io[f"e{e_i}_b2"]),
                                "b2l", slow=True)
                    p.wait("vector", "b2l", tb2)
                    p.drain("vector")
                    p.on("vector", lambda e: e.tensor_scalar(
                        nb2c[:, :], nb2c[:, :], -1.0, None, op0=ALU.mult))
                    p.drain("vector")
                    if e_i == 0:
                        p.do("vector", lambda e: e.tensor_copy(b2s[:, :],
                                                               nb2c[:, :]),
                             inc=f"{hw}nb")
                    else:
                        p.do("vector", lambda e: e.tensor_tensor(
                            b2s[:, :], b2s[:, :], nb2c[:, :], op=ALU.add),
                            inc=f"{hw}nb")

            for tR in range(2):
                for j in range(nj):
                    ws = w1sl[j % 2]
                    woff = R1 + 16384 + (j % 2) * 1024
                    if p.val(f"{hw}e1t") >= 2:
                        p.wait("gpsimd", f"{hw}e1t", p.val(f"{hw}e1t") - 1)
                    wt = stream_cast(p, ar, wcols(w1d, j * 128, (j + 1) * 128),
                                     ws, 8, 128)
                    bank = ar.ps[j % 2]
                    p.wait("tensor", "stgc", wt)
                    if p.val(f"{hw}e1") >= 2:
                        p.wait("tensor", f"{hw}e1", p.val(f"{hw}e1") - 1)
                    if shared and j == 0:
                        pass
                    for kc in range(8):
                        tt = p.do("tensor", lambda e, kc=kc, j=j, tR=tR, bank=bank,
                                  ws=ws:
                                  e.matmul(bank[:, :], ws[:, kc, :],
                                           gh_fm[:, kc, tR * 512:(tR + 1) * 512],
                                           start=(kc == 0), stop=(kc == 7)),
                                  inc=(f"{hw}e1t" if kc == 7 else None))
                    p.wait("scalar", f"{hw}e1t", tt)
                    p.wait("scalar", f"{hw}ftj", p.val(f"{hw}ftj"))
                    p.wait("scalar", f"{hw}ft", p.val(f"{hw}ft"))
                    bias = sb1c if shared else b1c
                    p.do("scalar", lambda e, j=j, bank=bank, bias=bias: e.activation(
                        out=gE1[:, j, :], in_=bank[:, :], func=AF.Gelu,
                        bias=bias[:, j:j + 1]), inc=f"{hw}e1")
                e1_done = p.val(f"{hw}e1")

                p.wait("tensor", f"{hw}e1", e1_done)
                if do_c and tR == 0:
                    p.wait("tensor", f"{hw}gb", gb_t)
                    p.wait("tensor", f"{hw}nb", p.val(f"{hw}nb"))
                for oR in range(2):
                    for j in range(nj):
                        ws2 = w2sl[j % 2]
                        w2off = R1 + 18432 + (j % 2) * 512
                        p.wait("gpsimd", f"{hw}ft", p.val(f"{hw}ft"))
                        if p.val(f"{hw}ftj") >= 2:
                            p.wait("gpsimd", f"{hw}ftj", p.val(f"{hw}ftj") - 1)
                        wt2 = stream_cast(
                            p, ar, w2d[j * 128:(j + 1) * 128,
                                       oR * 512:(oR + 1) * 512],
                            ws2, 1, 512)
                        p.wait("tensor", "stgc", wt2)
                        last_j = (j == nj - 1)
                        has_c = do_c and tR == 0
                        for tp in range(4):
                            if tp == 3 and last_j and not shared and not has_c:
                                inc = f"{hw}ft"
                            elif tp == 3 and not has_c:
                                inc = f"{hw}ftj"
                            else:
                                inc = None
                            p.do("tensor", lambda e, j=j, tp=tp, ws2=ws2,
                                 shared=shared, last_j=last_j:
                                 e.matmul(ar.ps[2 + tp][:, :],
                                          gE1[:, j, tp * 128:(tp + 1) * 128],
                                          ws2[:, :], start=(j == 0),
                                          stop=(last_j and not shared)),
                                 inc=inc)
                        if do_c and tR == 0:
                            if p.val(f"{hw}ca") >= 2:
                                p.wait("tensor", f"{hw}ca", p.val(f"{hw}ca") - 1)
                            for oc in range(4):
                                col = (j % 2) * 4 + oc
                                inc = f"{hw}ftj" if oc == 3 else None
                                p.do("tensor", lambda e, j=j, col=col, oc=oc,
                                     ws2=ws2:
                                     e.matmul(ar.ps[0][:, col:col + 1],
                                              ws2[:, oc * 128:(oc + 1) * 128],
                                              gb1[:, j:j + 1],
                                              start=True, stop=True),
                                     inc=inc)
                            p.wait("vector", f"{hw}ftj", p.val(f"{hw}ftj"))
                            p.drain("vector")
                            if j == 0 and oR == 0:
                                p.do("vector", lambda e, oR=oR: e.tensor_copy(
                                    cacc[:, oR * 4:(oR + 1) * 4],
                                    ar.ps[0][:, 0:4]), inc=f"{hw}ca")
                            elif j == 0:
                                p.do("vector", lambda e, oR=oR, j=j: e.tensor_copy(
                                    cacc[:, oR * 4:(oR + 1) * 4],
                                    ar.ps[0][:, (j % 2) * 4:(j % 2) * 4 + 4]),
                                    inc=f"{hw}ca")
                            else:
                                p.do("vector", lambda e, oR=oR, j=j: e.tensor_tensor(
                                    cacc[:, oR * 4:(oR + 1) * 4],
                                    cacc[:, oR * 4:(oR + 1) * 4],
                                    ar.ps[0][:, (j % 2) * 4:(j % 2) * 4 + 4],
                                    op=ALU.add), inc=f"{hw}ca")
                    if shared:
                        for tp in range(4):
                            tt = p.do("tensor", lambda e, tp=tp, oR=oR: e.matmul(
                                ar.ps[2 + tp][:, :], ones[:, :],
                                sb2row_bf[0:1, oR * 512:(oR + 1) * 512],
                                start=False, stop=True),
                                inc=(f"{hw}ft" if tp == 3 else None))
                    else:
                        tt = p.val(f"{hw}ft")
                    p.wait("vector", f"{hw}ft", tt)
                    p.drain("vector")
                    for tp in range(4):
                        i = tR * 4 + tp
                        bank = ar.ps[2 + tp]
                        aslice = acc[:, i, oR * 512:(oR + 1) * 512]
                        if shared:
                            p.do("vector", lambda e, bank=bank, aslice=aslice:
                                 e.scalar_tensor_tensor(aslice, bank[:, :], 0.1,
                                                        aslice, op0=ALU.mult,
                                                        op1=ALU.add),
                                 inc=f"{hw}bl")
                        elif e_i == 0:
                            p.do("vector", lambda e, bank=bank, aslice=aslice, i=i:
                                 e.tensor_scalar(aslice, bank[:, :],
                                                 w8[:, i * 8:i * 8 + 1], None,
                                                 op0=ALU.mult), inc=f"{hw}bl")
                        else:
                            p.do("vector", lambda e, bank=bank, aslice=aslice, i=i,
                                 e_i=e_i:
                                 e.scalar_tensor_tensor(
                                     aslice, bank[:, :],
                                     w8[:, i * 8 + e_i:i * 8 + e_i + 1], aslice,
                                     op0=ALU.mult, op1=ALU.add), inc=f"{hw}bl")
                    p.wait("tensor", f"{hw}bl", p.val(f"{hw}bl"))
                if do_c and tR == 0:
                    p.wait("scalar", f"{hw}nb", p.val(f"{hw}nb"))
                    p.wait("scalar", f"{hw}ca", p.val(f"{hw}ca"))
                    for col in range(8):
                        p.do("scalar", lambda e, col=col, e_i=e_i: e.activation(
                            out=ctile[:, col * 9 + e_i:col * 9 + e_i + 1],
                            in_=cacc[:, col:col + 1], func=AF.Identity,
                            scale=ar.negcol[:, :]), inc=f"{hw}ce")
                    p.wait("vector", f"{hw}ce", p.val(f"{hw}ce"))

        if half == 0:
            p.wait("vector", f"{hw}ce", p.val(f"{hw}ce"))
            p.wait("vector", "cmv", p.val("cmv"))
            p.drain("vector")
            for oc in range(8):
                p.do("vector", lambda e, oc=oc: e.reduce_sum(
                    ctile[:, oc * 9 + 8:oc * 9 + 9], ctile[:, oc * 9:oc * 9 + 8],
                    axis=AX.X), inc="cs")
            p.drain("vector")
            for oc in range(8):
                p.do("vector", lambda e, oc=oc: e.tensor_tensor(
                    ctile[:, oc * 9 + 8:oc * 9 + 9], ctile[:, oc * 9 + 8:oc * 9 + 9],
                    b2s[:, oc:oc + 1], op=ALU.add), inc="cs")
            p.drain("vector")
            cs = p.val("cs")
            p.wait("tensor", "cs", cs)
            for oc in range(8):
                tt = p.do("tensor", lambda e, oc=oc: e.transpose(
                    ar.ps[4][0:9, 0:128], ctile[:, oc * 9:oc * 9 + 9],
                    ar.identf[:, :]), inc="cmt")
                p.wait("vector", "cmt", tt)
                cv = p.do("vector", lambda e, oc=oc: e.tensor_copy(
                    cm10[0:9, oc * 128:(oc + 1) * 128], ar.ps[4][0:9, 0:128]),
                    inc="cmv")
                p.wait("tensor", "cmv", cv)
        p.wait("tensor", "cmv", p.val("cmv"))
        p.drain("vector")
        for i in range(NTH):
            for oR in range(2):
                bank = ar.ps[2 + (i * 2 + oR) % 4]
                tt = p.do("tensor", lambda e, i=i, oR=oR, bank=bank: e.matmul(
                    bank[:, :], w10t[0:16, i * 128:(i + 1) * 128],
                    cm10[0:16, oR * 512:(oR + 1) * 512], start=True, stop=True),
                    inc=f"{hw}cat")
                p.wait("vector", f"{hw}cat", tt)
                av = p.do("vector", lambda e, i=i, oR=oR, bank=bank:
                          e.tensor_tensor(acc[:, i, oR * 512:(oR + 1) * 512],
                                          acc[:, i, oR * 512:(oR + 1) * 512],
                                          bank[:, :], op=ALU.add), inc=f"{hw}cav")
                p.wait("tensor", f"{hw}cav", av)

        acc_bf = ar.bf(GH0, [128, 8, TH])
        acc_fm = ar.bf(R1, [128, 8, TH])
        p.wait("scalar", f"{hw}cav", p.val(f"{hw}cav"))
        p.wait("scalar", f"{hw}bl", p.val(f"{hw}bl"))
        for i in range(NTH):
            for c2 in range(2):
                p.do("scalar", lambda e, i=i, c2=c2: e.activation(
                    out=acc_bf[:, i, c2 * 512:(c2 + 1) * 512],
                    in_=acc[:, i, c2 * 512:(c2 + 1) * 512], func=AF.Identity),
                    inc=f"{hw}ab")
        abv = p.val(f"{hw}ab")
        p.wait("tensor", f"{hw}ab", abv)
        for i in range(NTH):
            bank = ar.pt[i % 2]
            if i >= 2:
                p.wait("tensor", f"{hw}afv", p.val(f"{hw}afv") - 1)
            for lc in range(8):
                tt = p.do("tensor", lambda e, i=i, lc=lc, bank=bank: e.transpose(
                    bank[:, lc * 128:(lc + 1) * 128],
                    acc_bf[:, i, lc * 128:(lc + 1) * 128], ar.ident[:, :]),
                    inc=(f"{hw}aft" if lc == 7 else None))
            p.wait("vector", f"{hw}aft", tt)
            dst = acc_fm[:, 0:8, i * 128:(i + 1) * 128]
            src = bank[:].rearrange("p (a b) -> p a b", a=8)
            p.do("vector", lambda e, dst=dst, src=src: e.tensor_copy(dst, src),
                 inc=f"{hw}afv")
        af_done = p.val(f"{hw}afv")
        p.wait("tensor", f"{hw}afv", af_done)
        p.wait("tensor", "mbv", rows_ready)

        p.wait("gpsimd", f"{hw}ft", p.val(f"{hw}ft"))
        p.wait("gpsimd", f"{hw}e1t", p.val(f"{hw}e1t"))
        p.wait("gpsimd", f"{hw}aft", p.val(f"{hw}aft"))
        upsl = [ar.bf(R1 + 8192 + k * 4096, [128, 8, 512]) for k in range(4)]
        xsl = [ar.f32(R1 + 24576 + k * 4096, [128, 2048]) for k in range(2)]
        csl = [ar.f32(R1 + 32768 + k * 4096, [128, 2048]) for k in range(2)]
        upt = None
        for oR in range(4):
            woff = R1 + 8192 + oR * 4096
            upt = stream_cast(p, ar, wcols(io["up_w"], oR * 512, (oR + 1) * 512),
                              upsl[oR], 8, 512)
        p.wait("tensor", "stgc", upt)
        p.wait("sync", f"{hw}cst0", p.val(f"{hw}cst0"))
        p.wait("sync", f"{hw}cst1", p.val(f"{hw}cst1"))
        yb = p.val(f"{hw}yv")
        for i in range(NTH):
            ypar = f"{hw}yst{i % 2}"
            xpar = f"{hw}xl{i % 2}"
            if i >= 2:
                p.wait("sync", ypar, p.val(ypar))
            xt = p.dma("sync", xsl[i % 2][:, :],
                       io["xin"][tok0 + i * 128: tok0 + (i + 1) * 128, :], xpar)
            ct = p.dma("sync", csl[i % 2][:, :],
                       io["core_scratch"][tok0 + i * 128: tok0 + (i + 1) * 128, :],
                       xpar)
            p.wait("vector", xpar, max(xt, ct))
            for oR in range(4):
                bank = ar.ps[2 + oR % 2]
                if p.val(f"{hw}yv") >= 2:
                    p.wait("tensor", f"{hw}yv", p.val(f"{hw}yv") - 1)
                for lc in range(8):
                    p.do("tensor", lambda e, i=i, lc=lc, oR=oR, bank=bank:
                         e.matmul(bank[:, :], acc_fm[:, lc, i * 128:(i + 1) * 128],
                                  upsl[oR][:, lc, :], start=(lc == 0), stop=False))
                tt = p.do("tensor", lambda e, bank=bank, oR=oR: e.matmul(
                    bank[:, :], ones[:, :],
                    ubrow_bf[0:1, oR * 512:(oR + 1) * 512],
                    start=False, stop=True), inc=f"{hw}ut")
                p.wait("vector", f"{hw}ut", tt)
                p.drain("vector")
                p.on("vector", lambda e, i=i, oR=oR, bank=bank: e.tensor_tensor(
                    ftmp[:, 0:512], bank[:, :],
                    csl[i % 2][:, oR * 512:(oR + 1) * 512], op=ALU.add))
                p.drain("vector")
                p.do("vector", lambda e, i=i, oR=oR: e.tensor_tensor(
                    csl[i % 2][:, oR * 512:(oR + 1) * 512], ftmp[:, 0:512],
                    xsl[i % 2][:, oR * 512:(oR + 1) * 512], op=ALU.add),
                    inc=f"{hw}yv")
            p.wait("gpsimd", f"{hw}yv", yb + (i + 1) * 4)
            p.dma("gpsimd", io["xout"][tok0 + i * 128: tok0 + (i + 1) * 128, :],
                  csl[i % 2][:, :], ypar)
        p.wait("gpsimd", f"{hw}yst0", p.val(f"{hw}yst0"))
        p.wait("gpsimd", f"{hw}yst1", p.val(f"{hw}yst1"))
        p.barrier()



MOE_WNAMES = (["down_w", "down_b", "up_w", "up_b", "router_w", "router_b",
               "shared_w1", "shared_b1", "shared_w2", "shared_b2",
               "core_w", "core_b"] +
              [f"e{k}_{n}" for k in range(8) for n in ("w1", "b1", "w2", "b2")])

MOE_SHAPES = {"down_w": [D, LDIM], "down_b": [LDIM], "up_w": [LDIM, D],
              "up_b": [D], "router_w": [D, 8], "router_b": [8],
              "shared_w1": [LDIM, LDIM], "shared_b1": [LDIM],
              "shared_w2": [LDIM, LDIM], "shared_b2": [LDIM],
              "core_w": [D, D], "core_b": [D]}
for _k in range(8):
    MOE_SHAPES[f"e{_k}_w1"] = [LDIM, HID_E[_k]]
    MOE_SHAPES[f"e{_k}_b1"] = [HID_E[_k]]
    MOE_SHAPES[f"e{_k}_w2"] = [HID_E[_k], LDIM]
    MOE_SHAPES[f"e{_k}_b2"] = [LDIM]


def build_moe():
    nc = bass.Bass("TRN2", target_bir_lowering=False)
    p = Prog(nc)
    ar = Arena(nc, p)
    io = {"xin": nc.dram_tensor("xin", [T, D], F32, kind="ExternalInput").ap(),
          "xout": nc.dram_tensor("xout", [T, D], F32, kind="ExternalOutput").ap(),
          "core_scratch": nc.dram_tensor("core_scratch", [T, D], F32,
                                         kind="Internal").ap()}
    for n in MOE_WNAMES:
        io[n] = nc.dram_tensor(n, MOE_SHAPES[n], F32, kind="ExternalInput").ap()
    emit_moe(p, ar, io)
    p.emit()
    return nc



def emit_l0b_a(p, ar, io):
    nc = p.nc
    ones = ar.ones1
    LN0 = 0
    HF0 = 32768
    TF0 = 40960
    CB0 = 40960

    ln_fm = ar.bf(LN0, [128, 16, T])
    h_fm = ar.bf(HF0, [128, 4, T])

    buc = ar.sm(128, [128, 16])
    bvc = ar.sm(144, [128, 16])
    bg1c = ar.sm(160, [128, 4])
    bg2c = ar.sm(164, [128, 16])
    cw = ar.sm(180, [128, 16 * 3])
    cb = ar.sm(232, [128, 16])
    hstat = ar.sm(248, [128, 16])
    p.dma("sync", buc[:, :], bcol(io["bu"]), "bl", slow=True)
    p.dma("sync", bvc[:, :], bcol(io["bv"]), "bl", slow=True)
    p.dma("sync", bg1c[:, :], bcol(io["bg1"]), "bl", slow=True)
    p.dma("sync", bg2c[:, :], bcol(io["bg2"]), "bl", slow=True)
    p.dma("sync", cb[:, :], bcol(io["conv_b"]), "bl", slow=True)
    cwsrc = io["conv_w"][:, 0, :].rearrange("(c p) k -> p c k", p=128)
    p.dma("sync", _reshape(cw[:, :], [128, 16, 3]), cwsrc, "bl", slow=True)
    bl_all = p.val("bl")

    emit_ln_fm(p, ar, io["xin"], 0, NT, ln_fm, CB0, "bL")
    p.barrier()
    halo_fm = ar.bf(HF0 + 8192 - 64, [128, 16, 2])
    hx = ar.f32(CB0, [128, 2048])
    ht = p.dma("sync", hx[0:2, :], io["halo"][:, :], "bhl")
    p.wait("vector", "bhl", ht)

    def hstats(e):
        e.reduce_sum(hstat[0:2, 0:1], hx[0:2, :], axis=AX.X)
        e.drain()
        e.tensor_scalar(hstat[0:2, 1:2], hstat[0:2, 0:1], 1.0 / D, None,
                        op0=ALU.mult)
        e.drain()
    p.on("vector", hstats)
    hv = p.do("vector", lambda e: e.tensor_scalar(
        hstat[0:2, 2:3], hstat[0:2, 1:2], -1.0, None, op0=ALU.mult), inc="bhv")
    p.wait("scalar", "bhv", hv)
    hs = p.do("scalar", lambda e: e.activation(
        out=hx[0:2, :], in_=hx[0:2, :], func=AF.Identity,
        bias=hstat[0:2, 2:3]), inc="bhs")
    p.wait("scalar", "bhs", hs)
    hs2 = p.do("scalar", lambda e: e.activation(
        out=ar.bf(CB0 + 8192, [128, 2048])[0:2, :], in_=hx[0:2, :],
        func=AF.Square, accum_out=hstat[0:2, 3:4]), inc="bhs")
    p.wait("vector", "bhs", hs2)

    def hstats2(e):
        e.tensor_scalar(hstat[0:2, 4:5], hstat[0:2, 3:4], 1.0 / D, EPS_LN,
                        op0=ALU.mult, op1=ALU.add)
        e.drain()
    p.on("vector", hstats2)
    hv2 = p.mark("vector", "bhv")
    p.wait("scalar", "bhv", hv2)
    hsq = p.do("scalar", lambda e: e.activation(
        out=hstat[0:2, 4:5], in_=hstat[0:2, 4:5], func=AF.Sqrt), inc="bhs")
    p.wait("vector", "bhs", hsq)
    p.drain("vector")
    hv2b = p.do("vector", lambda e: e.reciprocal(
        hstat[0:2, 5:6], hstat[0:2, 4:5]), inc="bhv")
    p.wait("scalar", "bhv", hv2b)
    hbf = ar.bf(CB0 + 8192, [128, 2048])
    hs3 = p.do("scalar", lambda e: e.activation(
        out=hbf[0:2, :], in_=hx[0:2, :], func=AF.Identity,
        scale=hstat[0:2, 5:6]), inc="bhs")
    p.wait("tensor", "bhs", hs3)
    for kc in range(NKC):
        tt = p.do("tensor", lambda e, kc=kc: e.transpose(
            ar.pt[0][:, 0:2], hbf[0:2, kc * 128:(kc + 1) * 128],
            ar.ident[0:2, 0:2]), inc="bht")
        p.wait("vector", "bht", tt)
        vv = p.do("vector", lambda e, kc=kc: e.tensor_copy(
            halo_fm[:, kc, :], ar.pt[0][:, 0:2]), inc="bhv2")
        p.wait("tensor", "bhv2", vv)
    p.barrier()

    p.wait("scalar", "bl", bl_all)
    wg1T = ar.bf(CB0, [128, 16, 512])
    for hc in range(4):
        st4 = ar.bf(CB0 + 8192, [128, 16, 128])
        p.wait("gpsimd", "bwt", p.val("bwt"))
        wt = stream_cast(p, ar,
                         io["Wg1"][hc * 128:(hc + 1) * 128, :].rearrange(
                             "p (kc c) -> p kc c", c=128),
                         st4, 16, 128)
        p.wait("tensor", "stgc", wt)
        for kc in range(NKC):
            tt = p.do("tensor", lambda e, kc=kc, st4=st4: e.transpose(
                ar.pt[kc % 2][:, 0:128], st4[:, kc, :], ar.ident[:, :]),
                inc="bwt")
            p.wait("vector", "bwt", tt)
            vv = p.do("vector", lambda e, kc=kc, hc=hc: e.tensor_copy(
                wg1T[:, kc, hc * 128:(hc + 1) * 128], ar.pt[kc % 2][:, 0:128]),
                inc="bwv")
            p.wait("tensor", "bwv", vv)
        for tf in range(4):
            bank = ar.ps[tf % 2]
            if p.val("bhm") >= 2:
                p.wait("tensor", "bhm", p.val("bhm") - 1)
            for kc in range(NKC):
                tt = p.do("tensor", lambda e, kc=kc, tf=tf, hc=hc, bank=bank:
                          e.matmul(bank[:, :], wg1T[:, kc, hc * 128:(hc + 1) * 128],
                                   ln_fm[:, kc, tf * 512:(tf + 1) * 512],
                                   start=(kc == 0), stop=(kc == NKC - 1)),
                          inc=("bhmt" if kc == NKC - 1 else None))
            p.wait("scalar", "bhmt", p.val("bhmt"))
            p.do("scalar", lambda e, hc=hc, tf=tf, bank=bank: e.activation(
                out=h_fm[:, hc, tf * 512:(tf + 1) * 512], in_=bank[:, :],
                func=AF.Relu, bias=bg1c[:, hc:hc + 1]), inc="bhm")
    h_done = p.val("bhm")

    wuT = ar.bf(CB0, [128, 16, 128])
    wvT = ar.bf(CB0 + 2048, [128, 16, 128])
    wg2T = ar.bf(CB0 + 4096, [128, 4, 128])
    st4b = ar.bf(CB0 + 4608, [128, 16, 128])
    gch = ar.f32(CB0 + 6656, [128, 2048])
    ufp = ar.f32(CB0 + 10752, [128, 2048])
    ugbf = ar.bf(CB0 + 14848, [128, 2048])
    tot = ar.sm(264, [128, 32])
    p.wait("tensor", "bhm", h_done)
    p.wait("gpsimd", "bhmt", p.val("bhmt"))
    p.wait("sync", "bhmt", p.val("bhmt"))
    for rc in range(NKC):
        for wname, dst in (("Wu", wuT), ("Wv", wvT), ("Wg2", wg2T)):
            nk = 4 if wname == "Wg2" else 16
            csz = 512 if wname == "Wg2" else 2048
            src = io[wname][rc * 128:(rc + 1) * 128, :].rearrange(
                "p (kc c) -> p kc c", c=128)
            p.wait("gpsimd", "bwv2", p.val("bwv2"))
            wt = stream_cast(p, ar, src, st4b[:, 0:nk, :], nk, 128)
            p.wait("tensor", "stgc", wt)
            for kc in range(nk):
                tt = p.do("tensor", lambda e, kc=kc: e.transpose(
                    ar.pt[kc % 2][:, 0:128], st4b[:, kc, :], ar.ident[:, :]),
                    inc="bwt2")
                p.wait("vector", "bwt2", tt)
                vv = p.do("vector", lambda e, kc=kc, dst=dst: e.tensor_copy(
                    dst[:, kc, :], ar.pt[kc % 2][:, 0:128]), inc="bwv2")
                p.wait("tensor", "bwv2", vv)
        for tf in range(4):
            bank = ar.ps[4]
            if tf > 0 or rc > 0:
                p.wait("tensor", "bgv", p.val("bgv"))
            for kc in range(4):
                tt = p.do("tensor", lambda e, kc=kc, tf=tf, bank=bank: e.matmul(
                    bank[:, :], wg2T[:, kc, :],
                    h_fm[:, kc, tf * 512:(tf + 1) * 512],
                    start=(kc == 0), stop=(kc == 3)),
                    inc=("bgt" if kc == 3 else None))
            p.wait("scalar", "bgt", p.val("bgt"))
            p.do("scalar", lambda e, tf=tf, rc=rc, bank=bank: e.activation(
                out=gch[:, tf * 512:(tf + 1) * 512], in_=bank[:, :],
                func=AF.Sigmoid, bias=bg2c[:, rc:rc + 1]), inc="bgv")
        g_done = p.val("bgv")
        for which, wT, bc, outdram in (("u", wuT, buc, io["ug"]),
                                       ("v", wvT, bvc, io["vg"])):
            for tf in range(4):
                bank = ar.ps[tf % 2]
                if p.val("buv") >= 2:
                    p.wait("tensor", "buv", p.val("buv") - 1)
                for kc in range(NKC):
                    tt = p.do("tensor", lambda e, kc=kc, tf=tf, bank=bank, wT=wT:
                              e.matmul(bank[:, :], wT[:, kc, :],
                                       ln_fm[:, kc, tf * 512:(tf + 1) * 512],
                                       start=(kc == 0), stop=(kc == NKC - 1)),
                              inc=("buvt" if kc == NKC - 1 else None))
                p.wait("scalar", "buvt", p.val("buvt"))
                p.wait("scalar", "bgv", g_done)
                sv = p.do("scalar", lambda e, tf=tf, rc=rc, bank=bank, bc=bc:
                          e.activation(out=ufp[:, tf * 512:(tf + 1) * 512],
                                       in_=bank[:, :], func=AF.Identity,
                                       bias=bc[:, rc:rc + 1]), inc="buvs")
                p.wait("vector", "buvs", sv)
                p.drain("vector")
                p.do("vector", lambda e, tf=tf: e.tensor_tensor(
                    ugbf[:, tf * 512:(tf + 1) * 512],
                    ufp[:, tf * 512:(tf + 1) * 512],
                    gch[:, tf * 512:(tf + 1) * 512], op=ALU.mult), inc="buv")
            uv_done = p.val("buv")
            if which == "u":
                p.wait("vector", "buv", uv_done)
                p.drain("vector")
                p.do("vector", lambda e, rc=rc: e.reduce_sum(
                    tot[:, rc:rc + 1], ugbf[:, :], axis=AX.X), inc="btv")
            p.wait("gpsimd", "buv", uv_done)
            if which == "u":
                p.wait("gpsimd", "btv", p.val("btv"))
            st = p.dma("gpsimd", outdram[rc], ugbf[:, :], f"bst{rc % 2}")
            p.wait("vector", f"bst{rc % 2}", st)
    tv = p.val("btv")
    p.wait("gpsimd", "btv", tv)
    p.dma("gpsimd", io["totals"][:, :], tot[:, 0:16], "btot", slow=True)
    p.wait("gpsimd", "btot", p.val("btot"))

    p.barrier()
    t_fm = ar.bf(TF0, [128, 16, T])
    p.wait("vector", "bst0", p.val("bst0"))
    p.wait("vector", "bst1", p.val("bst1"))
    p.drain("vector")
    for kc in range(NKC):
        def conv(e, kc=kc):
            w0 = cw[:, kc * 3:kc * 3 + 1]
            w1 = cw[:, kc * 3 + 1:kc * 3 + 2]
            w2 = cw[:, kc * 3 + 2:kc * 3 + 3]
            lnc = ln_fm[:, kc, :]
            t = t_fm[:, kc, :]
            e.tensor_scalar(t, lnc, w1, cb[:, kc:kc + 1], op0=ALU.mult,
                            op1=ALU.add)
            e.drain()
            e.scalar_tensor_tensor(t[:, 1:T], lnc[:, 0:T - 1], w0, t[:, 1:T],
                                   op0=ALU.mult, op1=ALU.add)
            e.drain()
            e.scalar_tensor_tensor(t[:, 0:1], halo_fm[:, kc, 0:1], w0, t[:, 0:1],
                                   op0=ALU.mult, op1=ALU.add)
            e.drain()
            e.scalar_tensor_tensor(t[:, 0:T - 1], lnc[:, 1:T], w2, t[:, 0:T - 1],
                                   op0=ALU.mult, op1=ALU.add)
            e.drain()
            e.scalar_tensor_tensor(t[:, T - 1:T], halo_fm[:, kc, 1:2], w2,
                                   t[:, T - 1:T], op0=ALU.mult, op1=ALU.add)
            e.drain()
        p.on("vector", conv)
    ct = p.mark("vector", "bcv")
    xs2 = [ar.f32(HF0 + i * 4096, [128, 2048]) for i in range(2)]
    ttm = ar.bf(0, [128, 2048])
    p.wait("tensor", "bcv", ct)
    p.wait("sync", "bcv", ct)
    for i in range(NT):
        xpar = f"bxl{i % 2}"
        if i >= 2:
            p.wait("sync", f"bbs{i % 2}", p.val(f"bbs{i % 2}"))
        xt = p.dma("sync", xs2[i % 2][:, :], io["xin"][i * 128:(i + 1) * 128, :],
                   xpar)
        p.drain("vector")
        for half2 in range(2):
            bank = ar.pt[half2]
            if i >= 1:
                p.wait("tensor", "bte", p.val("bte") - 1)
            for j8 in range(8):
                j = half2 * 8 + j8
                tt = p.do("tensor", lambda e, i=i, j=j, j8=j8, bank=bank:
                          e.transpose(bank[:, j8 * 128:(j8 + 1) * 128],
                                      t_fm[:, j, i * 128:(i + 1) * 128],
                                      ar.ident[:, :]),
                          inc=("btt" if j8 == 7 else None))
            p.wait("vector", "btt", p.val("btt"))
            p.do("vector", lambda e, half2=half2, bank=bank: e.tensor_copy(
                ttm[:, half2 * 1024:(half2 + 1) * 1024], bank[:, :]), inc="bte")
        p.wait("vector", xpar, xt)
        p.drain("vector")
        p.do("vector", lambda e, i=i: e.tensor_tensor(
            xs2[i % 2][:, :], xs2[i % 2][:, :], ttm[:, :], op=ALU.add),
            inc="bbv")
        bv_ = p.val("bbv")
        p.wait("gpsimd", "bbv", bv_)
        p.dma("gpsimd", io["base"][i * 128:(i + 1) * 128, :], xs2[i % 2][:, :],
              f"bbs{i % 2}")
    p.wait("gpsimd", "bbs0", p.val("bbs0"))
    p.wait("gpsimd", "bbs1", p.val("bbs1"))



def emit_l0b_b(p, ar, io):
    nc = p.nc
    G0 = 0
    W20 = 32768
    CB0 = 65536

    Gfm = ar.bf(G0, [128, 16, T])
    W2sb = ar.bf(W20, [128, 16, T])
    ldb = [ar.bf(CB0 + k * 2048, [128, 2048]) for k in range(4)]
    kcum = ar.bf(CB0 + 8192, [128, 2048])
    zbf = ar.stg[:, 3072:4096].bitcast(BF16)
    tin = ar.sm(0, [128, 16])
    ti = p.dma("sync", tin[:, :], io["totals_in"][:, :], "c0l", slow=True)

    st4 = ar.bf(CB0, [128, 16, 128])
    for rc in range(NKC):
        p.wait("gpsimd", "cwv", p.val("cwv"))
        wt = stream_cast(p, ar, wcols(io["Wu"], rc * 128, (rc + 1) * 128),
                         st4, 16, 128)
        p.wait("tensor", "stgc", wt)
        for dc in range(NKC):
            tt = p.do("tensor", lambda e, dc=dc: e.transpose(
                ar.pt[dc % 2][:, 0:128], st4[:, dc, :], ar.ident[:, :]),
                inc="cwt")
            p.wait("vector", "cwt", tt)
            vv = p.do("vector", lambda e, dc=dc, rc=rc: e.tensor_copy(
                W2sb[:, rc, dc * 128:(dc + 1) * 128], ar.pt[dc % 2][:, 0:128]),
                inc="cwv")
            p.wait("tensor", "cwv", vv)

    p.wait("sync", "cwv", p.val("cwv"))
    p.wait("vector", "stgc", p.val("stgc"))
    p.do("vector", lambda e: e.memset(zbf[:, :], 0.0), inc="c0v")
    z_done = p.val("c0v")
    p.wait("vector", "c0l", ti)
    p.wait("vector", "c0v", z_done)
    for rc in range(NKC):
        upar = f"cul{rc % 2}"
        if rc >= 2:
            p.wait("sync", "cgv", p.val("cgv") - 1)
        a_ug = ldb[(rc % 2) * 2]
        a_vg = ldb[(rc % 2) * 2 + 1]
        ut = p.dma("sync", a_ug[:, :], io["ug"][rc], upar)
        vt = p.dma("sync", a_vg[:, :], io["vg"][rc], upar)
        p.wait("vector", upar, vt)
        p.drain("vector")
        p.on("vector", lambda e, rc=rc, a_ug=a_ug: e.tensor_tensor_scan(
            kcum[:, :], a_ug[:, :], zbf[:, :], tin[:, rc:rc + 1],
            op0=ALU.add, op1=ALU.add))
        p.drain("vector")
        p.do("vector", lambda e, rc=rc, a_vg=a_vg: e.tensor_tensor(
            Gfm[:, rc, :], kcum[:, :], a_vg[:, :], op=ALU.mult), inc="cgv")
    g_done = p.val("cgv")

    xs2 = [ar.f32(CB0 + k * 4096, [128, 2048]) for k in range(2)]
    p.wait("tensor", "cgv", g_done)
    p.wait("sync", "cgv", g_done)
    p.wait("tensor", "cwv", p.val("cwv"))
    for i in range(NT):
        xpar = f"cxl{i % 2}"
        if i >= 2:
            p.wait("sync", f"cbs{i % 2}", p.val(f"cbs{i % 2}"))
        xt = p.dma("sync", xs2[i % 2][:, :], io["base"][i * 128:(i + 1) * 128, :],
                   xpar)
        p.wait("vector", xpar, xt)
        for dF in range(4):
            bank = ar.ps[2 + dF % 2]
            if p.val("cuv") >= 2:
                p.wait("tensor", "cuv", p.val("cuv") - 1)
            for rc in range(NKC):
                tt = p.do("tensor", lambda e, i=i, dF=dF, rc=rc, bank=bank:
                          e.matmul(bank[:, :], Gfm[:, rc, i * 128:(i + 1) * 128],
                                   W2sb[:, rc, dF * 512:(dF + 1) * 512],
                                   start=(rc == 0), stop=(rc == NKC - 1)),
                          inc=("cut" if rc == NKC - 1 else None))
            p.wait("vector", "cut", p.val("cut"))
            p.drain("vector")
            p.do("vector", lambda e, i=i, dF=dF, bank=bank: e.tensor_tensor(
                xs2[i % 2][:, dF * 512:(dF + 1) * 512],
                xs2[i % 2][:, dF * 512:(dF + 1) * 512], bank[:, :], op=ALU.add),
                inc="cuv")
        yv = p.val("cuv")
        p.wait("gpsimd", "cuv", yv)
        p.dma("gpsimd", io["xout"][i * 128:(i + 1) * 128, :], xs2[i % 2][:, :],
              f"cbs{i % 2}")
    p.wait("gpsimd", "cbs0", p.val("cbs0"))
    p.wait("gpsimd", "cbs1", p.val("cbs1"))


L0B_WSHAPES = {"Wu": [D, D], "bu": [D], "Wv": [D, D], "bv": [D],
               "Wg1": [512, D], "bg1": [512], "Wg2": [D, 512], "bg2": [D],
               "conv_w": [D, 1, 3], "conv_b": [D]}


def build_l0b_a():
    nc = bass.Bass("TRN2", target_bir_lowering=False)
    p = Prog(nc)
    ar = Arena(nc, p)
    io = {"xin": nc.dram_tensor("xin", [T, D], F32, kind="ExternalInput").ap(),
          "halo": nc.dram_tensor("halo", [2, D], F32, kind="ExternalInput").ap(),
          "ug": nc.dram_tensor("ug", [NKC, 128, T], BF16, kind="ExternalOutput").ap(),
          "vg": nc.dram_tensor("vg", [NKC, 128, T], BF16, kind="ExternalOutput").ap(),
          "base": nc.dram_tensor("base", [T, D], F32, kind="ExternalOutput").ap(),
          "totals": nc.dram_tensor("totals", [128, NKC], F32,
                                   kind="ExternalOutput").ap()}
    for n, s in L0B_WSHAPES.items():
        io[n] = nc.dram_tensor(n, s, F32, kind="ExternalInput").ap()
    emit_l0b_a(p, ar, io)
    p.emit()
    return nc


def build_l0b_b():
    nc = bass.Bass("TRN2", target_bir_lowering=False)
    p = Prog(nc)
    ar = Arena(nc, p)
    io = {"ug": nc.dram_tensor("ug", [NKC, 128, T], BF16, kind="ExternalInput").ap(),
          "vg": nc.dram_tensor("vg", [NKC, 128, T], BF16, kind="ExternalInput").ap(),
          "base": nc.dram_tensor("base", [T, D], F32, kind="ExternalInput").ap(),
          "totals_in": nc.dram_tensor("totals_in", [128, NKC], F32,
                                      kind="ExternalInput").ap(),
          "Wu": nc.dram_tensor("Wu", [D, D], F32, kind="ExternalInput").ap(),
          "xout": nc.dram_tensor("xout", [T, D], F32, kind="ExternalOutput").ap()}
    emit_l0b_b(p, ar, io)
    p.emit()
    return nc



NIT = 43


def emit_ghost_gather(p, ar, io):
    nc = p.nc
    lutc = ar.sm(0, [128, 16])
    lrow = ar.sm(16, [128, 16])
    t0 = p.dma("sync", lrow[0:1, :], brow(io["lut"]), "gl", slow=True)
    lrow_bf = ar.rowsbf[0:1, 0:16]
    p.wait("vector", "gl", t0)
    p.do("vector", lambda e: e.tensor_copy(lrow_bf, lrow[0:1, :]),
         inc="gv")
    gv = p.val("gv")
    p.wait("tensor", "gv", gv)
    tl = p.do("tensor", lambda e: e.matmul(
        ar.ps[4][:, 0:16], ar.ones1[:, :], lrow_bf,
        start=True, stop=True), inc="gt")
    p.wait("vector", "gt", tl)
    p.do("vector", lambda e: e.tensor_copy(lutc[:, :], ar.ps[4][:, 0:16]),
         inc="gv")
    lut_ready = p.val("gv")

    ixf = ar.bf(0, [128, 11008]).bitcast(I32)
    ixb = ar.bf(11008, [128, 5504])
    acc = [ar.bf(16512 + k * 256, [128, 256]) for k in range(4)]
    gth = [ar.bf(17536 + k * 256, [128, 256]) for k in range(4)]

    p.wait("vector", "gv", lut_ready)
    p.wait("gpsimd", "gv", lut_ready)

    def gather_mat(idx_name, out_name, nrow_tiles, ncols, outT):
        nct = ncols // 128
        for it in range(nrow_tiles):
            par = f"gil{it % 2}"
            p.wait("sync", "gcv0", p.val("gcv0"))
            lt = p.dma("sync", ixf[:, 0:ncols],
                       io[idx_name][it * 128:(it + 1) * 128, :], par)
            p.wait("vector", par, lt)
            p.wait("vector", f"ggg", p.val("ggg"))
            p.wait("vector", f"ggv", p.val("ggv"))
            p.drain("vector")
            cv = p.do("vector", lambda e, ncols=ncols: e.tensor_copy(
                ixb[:, 0:ncols], ixf[:, 0:ncols]), inc="gcv0")
            p.wait("vector", "gcv0", cv)
            p.wait("gpsimd", "gcv0", cv)
            for ct in range(nct):
                eng = "vector"
                a = acc[ct % 4]
                if p.val("gtt") >= 4:
                    p.wait(eng, "gtt", p.val("gtt") - 3)

                def gat(e, a=a, ct=ct):
                    src = ixb[:, ct * 128:(ct + 1) * 128]
                    eq = gth[2 + ct % 2]
                    e.tensor_scalar(a[:, 0:128], src, 0.0, lutc[:, 0:1],
                                    op0=ALU.is_equal, op1=ALU.mult)
                    for l in range(1, 16):
                        e.drain()
                        e.tensor_single_scalar(eq[:, 0:128], src, float(l),
                                               op=ALU.is_equal)
                        e.drain()
                        e.scalar_tensor_tensor(a[:, 0:128], eq[:, 0:128],
                                               lutc[:, l:l + 1], a[:, 0:128],
                                               op0=ALU.mult, op1=ALU.add)
                    e.drain()
                p.on(eng, gat)
                gg = p.mark(eng, "ggv" if eng == "vector" else "ggg")
                p.wait("tensor", "ggv" if eng == "vector" else "ggg", gg)
                tt = p.do("tensor", lambda e, a=a, ct=ct: e.transpose(
                    ar.pt[ct % 2][:, 0:128], a[:, 0:128], ar.ident[:, :]),
                    inc="gtt")
                p.wait("vector", "gtt", tt)
                spar = f"gst{ct % 2}"
                if p.val(spar) >= 16:
                    p.wait("vector", spar, p.val(spar))
                p.drain("vector")
                ev = p.do("vector", lambda e, ct=ct: e.tensor_copy(
                    gth[ct % 2][:, 0:128], ar.pt[ct % 2][:, 0:128]), inc="gev")
                p.wait("gpsimd", "gev", ev)
                dst = outT[ct, :, it * 128:(it + 1) * 128]
                p.dma("gpsimd", dst, gth[ct % 2][:, 0:128], spar)
    gather_mat("gidx", "wgt", NIT, 256, io["wgt"])
    gather_mat("uidx", "wut", NIT, 256, io["wut"])
    gather_mat("didx", "wdt", 2, 5504, io["wdt"])
    p.wait("gpsimd", "gst0", p.val("gst0"))
    p.wait("gpsimd", "gst1", p.val("gst1"))


def build_ghost_gather():
    nc = bass.Bass("TRN2", target_bir_lowering=False)
    p = Prog(nc)
    ar = Arena(nc, p)
    io = {"gidx": nc.dram_tensor("gidx", [5504, 256], I32, kind="ExternalInput").ap(),
          "uidx": nc.dram_tensor("uidx", [5504, 256], I32, kind="ExternalInput").ap(),
          "didx": nc.dram_tensor("didx", [256, 5504], I32, kind="ExternalInput").ap(),
          "lut": nc.dram_tensor("lut", [16], F32, kind="ExternalInput").ap(),
          "wgt": nc.dram_tensor("wgt", [2, 128, 5504], BF16, kind="ExternalOutput").ap(),
          "wut": nc.dram_tensor("wut", [2, 128, 5504], BF16, kind="ExternalOutput").ap(),
          "wdt": nc.dram_tensor("wdt", [NIT, 128, 256], BF16, kind="ExternalOutput").ap()}
    emit_ghost_gather(p, ar, io)
    p.emit()
    return nc



def emit_l1f(p, ar, io):
    nc = p.nc
    TR = 512
    NTR = T // TR
    LN0, HP0, XB0, WS0, GB0 = 0, 8192, 30720, 39040, 47232
    ln_fm = ar.bf(LN0, [128, 16, TR])
    hp = ar.bf(HP0, [128, NIT, TR])
    xsl = [ar.f32(XB0 + k * 2048, [128, 1024]) for k in range(4)]
    ybf = [ar.f32(XB0 + 8192 + k * 2048, [128, 1024]) for k in range(4)]
    wsl = [ar.bf(WS0 + k * 2048, [128, 16, 128]) for k in range(2)]
    wdsl = [ar.bf(WS0 + 4096 + k * 2048, [128, 2048]) for k in range(2)]
    gch = ar.bf(GB0, [128, TR])
    banks8 = ([b[:, :] for b in ar.ps[2:6]] +
              [ar.pt[0][:, :].bitcast(F32), ar.pt[1][:, :].bitcast(F32)] +
              [b[:, :] for b in ar.ps[0:2]])

    for tR in range(NTR):
        hw = f"f{tR}"
        emit_ln_fm(p, ar, io["xin"], tR * TR, TR // 128, ln_fm, HP0, f"fL{tR}")
        p.barrier()
        for j in range(NIT):
            for gu in range(2):
                wname = "wgt" if gu == 0 else "wut"
                ws = wsl[gu]
                p.wait("sync", f"{hw}gt", p.val(f"{hw}gt"))
                wt = p.dma("sync", ws[:, :, :],
                           io[wname][:, :, j * 128:(j + 1) * 128]
                           .rearrange("hc p c -> p hc c"), f"{hw}wl{gu}")
                p.wait("tensor", f"{hw}wl{gu}", wt)
                bank = ar.ps[gu]
                for kc in range(NKC):
                    tt = p.do("tensor", lambda e, kc=kc, ws=ws, bank=bank:
                              e.matmul(bank[:, :], ws[:, kc, :],
                                       ln_fm[:, kc, :], start=(kc == 0),
                                       stop=(kc == NKC - 1)),
                              inc=(f"{hw}gt" if kc == NKC - 1 else None))
                gt = p.val(f"{hw}gt")
                if gu == 0:
                    p.wait("scalar", f"{hw}gt", gt)
                    p.wait("scalar", f"{hw}hv", p.val(f"{hw}hv"))
                    p.do("scalar", lambda e, bank=bank: e.activation(
                        out=gch[:, :], in_=bank[:, :], func=AF.Silu),
                        inc=f"{hw}gs")
                else:
                    p.wait("vector", f"{hw}gt", gt)
                    p.wait("vector", f"{hw}gs", p.val(f"{hw}gs"))
                    p.drain("vector")
                    p.do("vector", lambda e, j=j, bank=bank: e.tensor_tensor(
                        hp[:, j, :], gch[:, :], bank[:, :], op=ALU.mult),
                        inc=f"{hw}hv")
            p.wait("tensor", f"{hw}gs", p.val(f"{hw}gs"))
            p.wait("tensor", f"{hw}hv", p.val(f"{hw}hv"))
        hp_done = p.val(f"{hw}hv")

        p.wait("tensor", f"{hw}hv", hp_done)
        for oH in range(2):
            for j in range(NIT):
                wd = wdsl[j % 2]
                if p.val(f"{hw}dt") >= 2:
                    p.wait("sync", f"{hw}dt", p.val(f"{hw}dt") - 1)
                wt2 = p.dma("sync", wd[:, 0:1024],
                            io["wdt"][j, :, oH * 1024:(oH + 1) * 1024],
                            f"{hw}wd{j % 2}")
                p.wait("tensor", f"{hw}wd{j % 2}", wt2)
                for tp in range(4):
                    for o2 in range(2):
                        last = (j == NIT - 1)
                        tt = p.do("tensor", lambda e, j=j, tp=tp, o2=o2, wd=wd:
                                  e.matmul(banks8[tp * 2 + o2],
                                           hp[:, j, tp * 128:(tp + 1) * 128],
                                           wd[:, o2 * 512:(o2 + 1) * 512],
                                           start=(j == 0), stop=(j == NIT - 1)),
                                  inc=(f"{hw}dt" if (tp == 3 and o2 == 1)
                                       else None))
            dt_done = p.val(f"{hw}dt")
            for tp in range(4):
                s = (oH % 2) * 2 + tp % 2
                xpar = f"{hw}xl{s}"
                p.wait("sync", f"{hw}ys{s}", p.val(f"{hw}ys{s}"))
                xt = p.dma("sync", xsl[s][:, :],
                           io["xin"][tR * TR + tp * 128: tR * TR + (tp + 1) * 128,
                                     oH * 1024:(oH + 1) * 1024], xpar)
                p.wait("vector", xpar, xt)
                p.wait("vector", f"{hw}dt", dt_done)
                p.wait("vector", f"{hw}ys{s}", p.val(f"{hw}ys{s}"))
                p.drain("vector")
                yb = ybf[s]
                p.on("vector", lambda e, tp=tp, yb=yb, s=s: e.tensor_tensor(
                    yb[:, 0:512], xsl[s][:, 0:512], banks8[tp * 2],
                    op=ALU.add))
                yv = p.do("vector", lambda e, tp=tp, yb=yb, s=s: e.tensor_tensor(
                    yb[:, 512:1024], xsl[s][:, 512:1024],
                    banks8[tp * 2 + 1], op=ALU.add), inc=f"{hw}yv")
                p.wait("gpsimd", f"{hw}yv", yv)
                p.dma("gpsimd",
                      io["xout"][tR * TR + tp * 128: tR * TR + (tp + 1) * 128,
                                 oH * 1024:(oH + 1) * 1024],
                      yb[:, :], f"{hw}ys{s}")
            p.wait("tensor", f"{hw}yv", p.val(f"{hw}yv"))
        for s in range(4):
            p.wait("gpsimd", f"{hw}ys{s}", p.val(f"{hw}ys{s}"))
        p.barrier()


GHOST_IO_SHAPES = {"wgt": [16, 128, 5504], "wut": [16, 128, 5504],
                   "wdt": [NIT, 128, 2048]}


def build_l1f():
    nc = bass.Bass("TRN2", target_bir_lowering=False)
    p = Prog(nc)
    ar = Arena(nc, p)
    io = {"xin": nc.dram_tensor("xin", [T, D], F32, kind="ExternalInput").ap(),
          "xout": nc.dram_tensor("xout", [T, D], F32, kind="ExternalOutput").ap()}
    for n, s in GHOST_IO_SHAPES.items():
        io[n] = nc.dram_tensor(n, s, BF16, kind="ExternalInput").ap()
    emit_l1f(p, ar, io)
    p.emit()
    return nc



def emit_wT_rows(p, ar, io_w, r0, wT, nkc, tag, st_off):
    st = ar.bf(st_off, [128, nkc, 128])
    p.wait("gpsimd", f"{tag}v", p.val(f"{tag}v"))
    wt = stream_cast(p, ar,
                     io_w[r0:r0 + 128, :].rearrange("p (kc c) -> p kc c", c=128),
                     st, nkc, 128)
    p.wait("tensor", "stgc", wt)
    for kc in range(nkc):
        tt = p.do("tensor", lambda e, kc=kc: e.transpose(
            ar.pt[kc % 2][:, 0:128], st[:, kc, :], ar.ident[:, :]), inc=f"{tag}t")
        p.wait("vector", f"{tag}t", tt)
        vv = p.do("vector", lambda e, kc=kc: e.tensor_copy(
            wT[:, kc, :], ar.pt[kc % 2][:, 0:128]), inc=f"{tag}v")
        p.wait("tensor", f"{tag}v", vv)
    return p.val(f"{tag}v")



def emit_l1a_a(p, ar, io):
    nc = p.nc
    SCALE = float(1.0 / np.sqrt(np.sqrt(128.0)))
    LN0 = 0
    WT0 = 32768
    ST0 = 36864
    EV0 = 40960
    ln_fm = ar.bf(LN0, [128, 16, T])
    emit_ln_fm(p, ar, io["xin"], 0, NT, ln_fm, WT0, "aL")
    p.barrier()

    bq = ar.sm(0, [128, 16])
    bk = ar.sm(16, [128, 16])
    bv = ar.sm(32, [128, 16])
    t1 = p.dma("sync", bq[:, :], bcol(io["bq"]), "abl", slow=True)
    t2 = p.dma("sync", bk[:, :], bcol(io["bk"]), "abl", slow=True)
    t3 = p.dma("sync", bv[:, :], bcol(io["bv"]), "abl", slow=True)
    bvrow = ar.rowsbf[0:1, 0:2048]
    t4 = p.dma("sync", ar.rows[0:1, 0:2048], brow(io["bv"]), "abl")
    bl_all = p.val("abl")
    p.wait("vector", "abl", bl_all)
    p.do("vector", lambda e: e.tensor_copy(bvrow, ar.rows[0:1, 0:2048]),
         inc="abv")
    p.drain("vector")
    p.on("vector", lambda e: e.tensor_scalar(bq[:, :], bq[:, :], SCALE, None,
                                             op0=ALU.mult))
    p.do("vector", lambda e: e.tensor_scalar(bk[:, :], bk[:, :], SCALE, None,
                                             op0=ALU.mult), inc="abv")
    rows_ready = p.val("abv")
    p.wait("scalar", "abv", rows_ready)
    p.wait("tensor", "abv", rows_ready)

    wT = [ar.bf(WT0 + k * 2048, [128, 16, 128]) for k in range(2)]
    evb = [ar.bf(EV0 + k * 2048, [128, 2048]) for k in range(2)]
    for h in range(16):
        for wi, (wn, bc, outd) in enumerate((("Wq", bq, io["qf"]),
                                             ("Wk", bk, io["kf"]))):
            wslot = wT[wi % 2]
            p.wait("gpsimd", "amt", p.val("amt"))
            emit_wT_rows(p, ar, io[wn], h * 128, wslot, 16, "awT", ST0)
            eslot = evb[wi % 2]
            for tf in range(4):
                bank = ar.ps[tf % 2]
                if p.val("ams") >= 2:
                    p.wait("tensor", "ams", p.val("ams") - 1)
                for kc in range(NKC):
                    tt = p.do("tensor", lambda e, kc=kc, tf=tf, wslot=wslot,
                              bank=bank:
                              e.matmul(bank[:, :], wslot[:, kc, :],
                                       ln_fm[:, kc, tf * 512:(tf + 1) * 512],
                                       start=(kc == 0), stop=(kc == NKC - 1)),
                              inc=("amt" if kc == NKC - 1 else None))
                p.wait("scalar", "amt", p.val("amt"))
                if tf == 0:
                    p.wait("scalar", f"ast{wi % 2}", p.val(f"ast{wi % 2}"))
                p.do("scalar", lambda e, tf=tf, h=h, bank=bank, bc=bc,
                     eslot=eslot: e.activation(
                         out=eslot[:, tf * 512:(tf + 1) * 512], in_=bank[:, :],
                         func=AF.Identity, bias=bc[:, h:h + 1], scale=SCALE),
                     inc="ams")
            sm = p.val("ams")
            p.wait("gpsimd", "ams", sm)
            p.dma("gpsimd", outd[h], eslot[:, :], f"ast{wi % 2}")
        wslot = wT[0]
        p.wait("gpsimd", "amt", p.val("amt"))
        p.wait("gpsimd", "avt", p.val("avt"))
        emit_wT_rows(p, ar, io["Wv"], h * 128, wslot, 16, "awT", ST0)
        p.wait("scalar", "ast0", p.val("ast0"))
        p.wait("scalar", "ast1", p.val("ast1"))
        for i in range(NT):
            bank = ar.ps[2 + i % 2]
            if p.val("avs") >= 2:
                p.wait("tensor", "avs", p.val("avs") - 1)
            for kc in range(NKC):
                p.do("tensor", lambda e, kc=kc, i=i, wslot=wslot, bank=bank:
                     e.matmul(bank[:, 0:128], ln_fm[:, kc, i * 128:(i + 1) * 128],
                              wslot[:, kc, :], start=(kc == 0), stop=False))
            tt = p.do("tensor", lambda e, bank=bank, h=h: e.matmul(
                bank[:, 0:128], ar.ones1[:, :], bvrow[0:1, h * 128:(h + 1) * 128],
                start=False, stop=True), inc="avt")
            p.wait("scalar", "avt", tt)
            p.wait("scalar", f"asv{i % 2}", p.val(f"asv{i % 2}"))
            sv = p.do("scalar", lambda e, i=i, bank=bank: e.activation(
                out=evb[i % 2][:, 0:128], in_=bank[:, 0:128], func=AF.Identity),
                inc="avs")
            p.wait("gpsimd", "avs", sv)
            p.dma("gpsimd", io["vt"][h, i * 128:(i + 1) * 128, :],
                  evb[i % 2][:, 0:128], f"asv{i % 2}")
        p.wait("scalar", "asv0", p.val("asv0"))
        p.wait("scalar", "asv1", p.val("asv1"))
    p.wait("gpsimd", "ast0", p.val("ast0"))
    p.wait("gpsimd", "ast1", p.val("ast1"))
    p.wait("gpsimd", "asv0", p.val("asv0"))
    p.wait("gpsimd", "asv1", p.val("asv1"))



SFULL = 4096


def emit_l1a_b(p, ar, io):
    nc = p.nc
    Q0, K0, V0, O0, MSK0, PT0_, OF0 = 0, 8192, 16384, 24576, 24576 + 8192, 41472, 45568
    qb = ar.bf(Q0, [128, SFULL])
    kb = ar.bf(K0, [128, SFULL])
    vb = ar.bf(V0, [128, 32, 128])
    ob = ar.bf(O0, [128, SFULL])
    msk = ar.f32(MSK0, [128, 4, 512])
    ptb = ar.bf(PT0_, [128, 4, 128])
    Of32 = ar.f32(OF0, [128, 128])
    Ssb = ar.f32(OF0 + 256, [128, 512])
    Pbf = ar.bf(OF0 + 1536, [128, 512])
    stat = ar.sm(0, [128, 8])

    for midx in range(4):
        p.do("gpsimd", lambda e, midx=midx: e.iota(
            msk[:, midx, :], [[1, 512]], base=-midx * 128, channel_multiplier=-1,
            allow_small_or_imprecise_dtypes=True), inc="fmg")
    mg = p.val("fmg")
    p.wait("vector", "fmg", mg)

    def mkmask(e):
        for midx in range(4):
            e.tensor_single_scalar(msk[:, midx, :], msk[:, midx, :], 0.5,
                                   op=ALU.is_gt)
            e.drain()
            e.tensor_scalar(msk[:, midx, :], msk[:, midx, :], -1e30, None,
                            op0=ALU.mult)
            e.drain()
    p.on("vector", mkmask)
    mv = p.mark("vector", "fmv")
    p.wait("scalar", "fmv", mv)
    p.wait("tensor", "fmv", mv)

    for u in range(8):
        lq = p.dma("sync", qb[:, :], io["qf"][u], "fq")
        lk = p.dma("sync", kb[:, :], io["kf"][u], "fk")
        lv = p.dma("sync", vb[:, :, :],
                   io["vt"][u].rearrange("(a p) c -> p a c", p=128), "fv")
        p.wait("tensor", "fq", lq)
        p.wait("tensor", "fk", lk)
        p.wait("tensor", "fv", lv)
        for qt in range(32):
            nkb = qt // 4 + 1
            for kbi in range(nkb):
                diag = (kbi == qt // 4)
                bank = ar.ps[kbi % 2]
                if p.val("fSv") >= 2:
                    p.wait("tensor", "fSv", p.val("fSv") - 1)
                tS = p.do("tensor", lambda e, qt=qt, kbi=kbi, bank=bank:
                          e.matmul(bank[:, :], qb[:, qt * 128:(qt + 1) * 128],
                                   kb[:, kbi * 512:(kbi + 1) * 512],
                                   start=True, stop=True), inc="fSt")
                p.wait("vector", "fSt", tS)
                p.drain("vector")
                if diag:
                    midx = qt % 4
                    p.do("vector", lambda e, bank=bank, midx=midx: e.tensor_tensor(
                        Ssb[:, :], bank[:, :], msk[:, midx, :], op=ALU.add),
                        inc="fSv")
                else:
                    p.do("vector", lambda e, bank=bank: e.tensor_copy(
                        Ssb[:, :], bank[:, :]), inc="fSv")
                p.drain("vector")

                def vstats(e, first=(kbi == 0)):
                    e.reduce_max(stat[:, 2:3], Ssb[:, :], axis=AX.X)
                    e.drain()
                    if first:
                        e.tensor_copy(stat[:, 0:1], stat[:, 2:3])
                    else:
                        e.tensor_tensor(stat[:, 2:3], stat[:, 0:1], stat[:, 2:3],
                                        op=ALU.max)
                        e.drain()
                        e.tensor_tensor(stat[:, 3:4], stat[:, 0:1], stat[:, 2:3],
                                        op=ALU.subtract)
                        e.drain()
                        e.tensor_copy(stat[:, 0:1], stat[:, 2:3])
                    e.drain()
                    e.tensor_scalar(stat[:, 5:6], stat[:, 0:1], -1.0, None,
                                    op0=ALU.mult)
                    e.drain()
                p.on("vector", vstats)
                vs = p.mark("vector", "fvs")
                p.wait("scalar", "fvs", vs)
                p.wait("scalar", "fpt", p.val("fpt"))
                sp = p.do("scalar", lambda e: e.activation(
                    out=Pbf[:, :], in_=Ssb[:, :], func=AF.Exp,
                    bias=stat[:, 5:6], accum_out=stat[:, 4:5]), inc="fps")
                if kbi > 0:
                    sp2 = p.do("scalar", lambda e: e.activation(
                        out=stat[:, 3:4], in_=stat[:, 3:4], func=AF.Exp),
                        inc="fps")
                p.wait("vector", "fps", p.val("fps"))

                def lup(e, first=(kbi == 0)):
                    if first:
                        e.tensor_copy(stat[:, 1:2], stat[:, 4:5])
                    else:
                        e.scalar_tensor_tensor(stat[:, 1:2], stat[:, 1:2],
                                               stat[:, 3:4], stat[:, 4:5],
                                               op0=ALU.mult, op1=ALU.add)
                    e.drain()
                p.on("vector", lup)
                p.mark("vector", "flv")
                p.wait("tensor", "fps", p.val("fps") - (1 if kbi > 0 else 0))
                bankT = ar.pt[kbi % 2]
                if p.val("fpv") >= 2:
                    p.wait("tensor", "fpv", p.val("fpv") - 1)
                for t4 in range(4):
                    tt = p.do("tensor", lambda e, t4=t4, bankT=bankT: e.transpose(
                        bankT[:, t4 * 128:(t4 + 1) * 128],
                        Pbf[:, t4 * 128:(t4 + 1) * 128], ar.ident[:, :]),
                        inc=("fpt" if t4 == 3 else None))
                p.wait("vector", "fpt", p.val("fpt"))
                p.wait("vector", "fot", p.val("fot"))
                p.drain("vector")
                pv = p.do("vector", lambda e, bankT=bankT: e.tensor_copy(
                    ptb[:, :, :],
                    bankT[:, 0:512].rearrange("p (a b) -> p a b", a=4)),
                    inc="fpv")
                p.wait("tensor", "fpv", pv)
                bankAV = ar.ps[2 + kbi % 2]
                if p.val("fov") >= 2:
                    p.wait("tensor", "fov", p.val("fov") - 1)
                for t4 in range(4):
                    tt = p.do("tensor", lambda e, t4=t4, kbi=kbi, bankAV=bankAV:
                              e.matmul(bankAV[:, 0:128], ptb[:, t4, :],
                                       vb[:, kbi * 4 + t4, :],
                                       start=(t4 == 0), stop=(t4 == 3)),
                              inc=("fot" if t4 == 3 else None))
                ot = p.val("fot")
                p.wait("vector", "fot", ot)
                p.wait("vector", "flv", p.val("flv"))
                p.drain("vector")
                if kbi == 0:
                    p.do("vector", lambda e, bankAV=bankAV: e.tensor_copy(
                        Of32[:, :], bankAV[:, 0:128]), inc="fov")
                else:
                    p.do("vector", lambda e, bankAV=bankAV: e.scalar_tensor_tensor(
                        Of32[:, :], Of32[:, :], stat[:, 3:4], bankAV[:, 0:128],
                        op0=ALU.mult, op1=ALU.add), inc="fov")
            fo = p.val("fov")
            p.wait("vector", "fov", fo)
            p.drain("vector")
            p.on("vector", lambda e: e.reciprocal(stat[:, 6:7], stat[:, 1:2]))
            p.drain("vector")
            fv2 = p.do("vector", lambda e: e.tensor_scalar(
                Pbf[:, 0:128], Of32[:, :], stat[:, 6:7], None, op0=ALU.mult),
                inc="ffv")
            p.wait("tensor", "ffv", fv2)
            tt = p.do("tensor", lambda e: e.transpose(
                ar.pt[0][:, 0:128], Pbf[:, 0:128], ar.ident[:, :]), inc="fft")
            p.wait("vector", "fft", tt)
            p.drain("vector")
            ev = p.do("vector", lambda e, qt=qt: e.tensor_copy(
                ob[:, qt * 128:(qt + 1) * 128], ar.pt[0][:, 0:128]), inc="fev")
            p.wait("scalar", "fft", p.val("fft"))
        evv = p.val("fev")
        p.wait("gpsimd", "fev", evv)
        st = p.dma("gpsimd", io["of"][u], ob[:, :], "fst")
        p.wait("sync", "fst", st)
    p.wait("gpsimd", "fst", p.val("fst"))



def emit_l1a_c(p, ar, io):
    OF0, WT0, ST0, XB0 = 0, 32768, 65536, 68608
    ofm = ar.bf(OF0, [128, 16, T])
    WoT = ar.bf(WT0, [128, 16, T])
    xsl = [ar.f32(XB0 + k * 2048, [128, 1024]) for k in range(2)]
    ybf = [ar.f32(XB0 + 4096 + k * 2048, [128, 1024]) for k in range(2)]
    lo = p.dma("sync", ofm[:, :, :], io["ofm"][:, :, :].rearrange("a p c -> p a c"),
               "col")
    t4 = p.dma("sync", ar.rows[0:1, 0:2048], brow(io["bo"]), "col")
    borow = ar.rowsbf[0:1, 0:2048]
    p.wait("vector", "col", p.val("col"))
    p.do("vector", lambda e: e.tensor_copy(borow, ar.rows[0:1, 0:2048]),
         inc="cov")
    bo_ready = p.val("cov")
    for rc in range(NKC):
        emit_wT_rows(p, ar, io["Wo"], rc * 128, 
                     _reshape(WoT[:, :, rc * 128:(rc + 1) * 128], [128, 16, 128])
                     if False else WoT[:, :, rc * 128:(rc + 1) * 128],
                     16, "cwT", ST0)
    p.wait("tensor", "cov", bo_ready)
    p.wait("tensor", "cwTv", p.val("cwTv"))
    for i in range(NT):
        for oH in range(2):
            s = oH
            xpar = f"cxl{s}"
            p.wait("sync", f"cys{s}", p.val(f"cys{s}"))
            xt = p.dma("sync", xsl[s][:, :],
                       io["xin"][i * 128:(i + 1) * 128,
                                 oH * 1024:(oH + 1) * 1024], xpar)
            for o2 in range(2):
                bank = ar.ps[2 + o2]
                oc = oH * 1024 + o2 * 512
                for kc in range(NKC):
                    p.do("tensor", lambda e, kc=kc, i=i, oc=oc, bank=bank:
                         e.matmul(bank[:, :], ofm[:, kc, i * 128:(i + 1) * 128],
                                  WoT[:, kc, oc:oc + 512],
                                  start=(kc == 0), stop=False))
                tt = p.do("tensor", lambda e, bank=bank, oc=oc: e.matmul(
                    bank[:, :], ar.ones1[:, :], borow[0:1, oc:oc + 512],
                    start=False, stop=True), inc="cmt")
            mt = p.val("cmt")
            p.wait("vector", xpar, xt)
            p.wait("vector", "cmt", mt)
            p.wait("vector", f"cys{s}", p.val(f"cys{s}"))
            p.drain("vector")
            p.on("vector", lambda e, s=s: e.tensor_tensor(
                ybf[s][:, 0:512], xsl[s][:, 0:512], ar.ps[2][:, :], op=ALU.add))
            yv = p.do("vector", lambda e, s=s: e.tensor_tensor(
                ybf[s][:, 512:1024], xsl[s][:, 512:1024], ar.ps[3][:, :],
                op=ALU.add), inc="cyv")
            p.wait("gpsimd", "cyv", yv)
            p.dma("gpsimd", io["xout"][i * 128:(i + 1) * 128,
                                       oH * 1024:(oH + 1) * 1024],
                  ybf[s][:, :], f"cys{s}")
            p.wait("tensor", "cyv", p.val("cyv"))
    p.wait("gpsimd", "cys0", p.val("cys0"))
    p.wait("gpsimd", "cys1", p.val("cys1"))


L1A_WSHAPES = {"Wq": [D, D], "bq": [D], "Wk": [D, D], "bk": [D],
               "Wv": [D, D], "bv": [D], "Wo": [D, D], "bo": [D]}


def build_l1a_a():
    nc = bass.Bass("TRN2", target_bir_lowering=False)
    p = Prog(nc)
    ar = Arena(nc, p)
    io = {"xin": nc.dram_tensor("xin", [T, D], F32, kind="ExternalInput").ap(),
          "qf": nc.dram_tensor("qf", [16, 128, T], BF16, kind="ExternalOutput").ap(),
          "kf": nc.dram_tensor("kf", [16, 128, T], BF16, kind="ExternalOutput").ap(),
          "vt": nc.dram_tensor("vt", [16, T, 128], BF16, kind="ExternalOutput").ap()}
    for n in ("Wq", "bq", "Wk", "bk", "Wv", "bv"):
        io[n] = nc.dram_tensor(n, L1A_WSHAPES[n], F32, kind="ExternalInput").ap()
    emit_l1a_a(p, ar, io)
    p.emit()
    return nc


def build_l1a_b():
    nc = bass.Bass("TRN2", target_bir_lowering=False)
    p = Prog(nc)
    ar = Arena(nc, p)
    io = {"qf": nc.dram_tensor("qf", [8, 128, SFULL], BF16, kind="ExternalInput").ap(),
          "kf": nc.dram_tensor("kf", [8, 128, SFULL], BF16, kind="ExternalInput").ap(),
          "vt": nc.dram_tensor("vt", [8, SFULL, 128], BF16, kind="ExternalInput").ap(),
          "of": nc.dram_tensor("of", [8, 128, SFULL], BF16, kind="ExternalOutput").ap()}
    emit_l1a_b(p, ar, io)
    p.emit()
    return nc


def build_l1a_c():
    nc = bass.Bass("TRN2", target_bir_lowering=False)
    p = Prog(nc)
    ar = Arena(nc, p)
    io = {"xin": nc.dram_tensor("xin", [T, D], F32, kind="ExternalInput").ap(),
          "ofm": nc.dram_tensor("ofm", [16, 128, T], BF16, kind="ExternalInput").ap(),
          "Wo": nc.dram_tensor("Wo", [D, D], F32, kind="ExternalInput").ap(),
          "bo": nc.dram_tensor("bo", [D], F32, kind="ExternalInput").ap(),
          "xout": nc.dram_tensor("xout", [T, D], F32, kind="ExternalOutput").ap()}
    emit_l1a_c(p, ar, io)
    p.emit()
    return nc



def emit_proj_nat_heads(p, ar, io, ln_fm, specs, tag):
    WT0, EV0 = 32768, 36864
    wsl = [ar.bf(WT0 + k * 2048, [128, 16, 128]) for k in range(2)]
    evb = [ar.bf(EV0 + k * 2048, [128, 2048]) for k in range(2)]
    for h in range(16):
        for si, (wname, colbase, func, out_fm, out_tm) in enumerate(specs):
            wslot = wsl[si % 2]
            c0 = colbase + h * 128
            p.wait("gpsimd", f"{tag}mt", p.val(f"{tag}mt"))
            p.wait("gpsimd", f"{tag}vt", p.val(f"{tag}vt"))
            wt = stream_cast(p, ar, wcols(io[wname], c0, c0 + 128), wslot, 16, 128)
            p.wait("tensor", "stgc", wt)
            if out_fm is not None:
                eslot = evb[si % 2]
                for tf in range(4):
                    bank = ar.ps[tf % 2]
                    if p.val(f"{tag}ms") >= 2:
                        p.wait("tensor", f"{tag}ms", p.val(f"{tag}ms") - 1)
                    for kc in range(NKC):
                        p.do("tensor", lambda e, kc=kc, tf=tf, wslot=wslot,
                             bank=bank:
                             e.matmul(bank[:, :], wslot[:, kc, :],
                                      ln_fm[:, kc, tf * 512:(tf + 1) * 512],
                                      start=(kc == 0), stop=(kc == NKC - 1)),
                             inc=(f"{tag}mt" if kc == NKC - 1 else None))
                    p.wait("scalar", f"{tag}mt", p.val(f"{tag}mt"))
                    if tf == 0:
                        p.wait("scalar", f"{tag}st{si % 2}",
                               p.val(f"{tag}st{si % 2}"))
                    p.do("scalar", lambda e, tf=tf, bank=bank, eslot=eslot,
                         func=func: e.activation(
                             out=eslot[:, tf * 512:(tf + 1) * 512],
                             in_=bank[:, :], func=func), inc=f"{tag}ms")
                sm = p.val(f"{tag}ms")
                p.wait("gpsimd", f"{tag}ms", sm)
                p.dma("gpsimd", out_fm[h], eslot[:, :], f"{tag}st{si % 2}")
            else:
                p.wait("scalar", f"{tag}st0", p.val(f"{tag}st0"))
                p.wait("scalar", f"{tag}st1", p.val(f"{tag}st1"))
                for i in range(NT):
                    bank = ar.ps[2 + i % 2]
                    if p.val(f"{tag}vs") >= 2:
                        p.wait("tensor", f"{tag}vs", p.val(f"{tag}vs") - 1)
                    for kc in range(NKC):
                        tt = p.do("tensor", lambda e, kc=kc, i=i, wslot=wslot,
                                  bank=bank:
                                  e.matmul(bank[:, 0:128],
                                           ln_fm[:, kc, i * 128:(i + 1) * 128],
                                           wslot[:, kc, :], start=(kc == 0),
                                           stop=(kc == NKC - 1)),
                                  inc=(f"{tag}vt" if kc == NKC - 1 else None))
                    p.wait("scalar", f"{tag}vt", p.val(f"{tag}vt"))
                    p.wait("scalar", f"{tag}sv{i % 2}", p.val(f"{tag}sv{i % 2}"))
                    sv = p.do("scalar", lambda e, i=i, bank=bank, func=func:
                              e.activation(out=evb[i % 2][:, 0:128],
                                           in_=bank[:, 0:128], func=func),
                              inc=f"{tag}vs")
                    p.wait("gpsimd", f"{tag}vs", sv)
                    p.dma("gpsimd", out_tm[h, i * 128:(i + 1) * 128, :],
                          evb[i % 2][:, 0:128], f"{tag}sv{i % 2}")
                p.wait("scalar", f"{tag}sv0", p.val(f"{tag}sv0"))
                p.wait("scalar", f"{tag}sv1", p.val(f"{tag}sv1"))
        p.wait("scalar", f"{tag}st0", p.val(f"{tag}st0"))
        p.wait("scalar", f"{tag}st1", p.val(f"{tag}st1"))
    for s in ("st0", "st1", "sv0", "sv1"):
        p.wait("gpsimd", f"{tag}{s}", p.val(f"{tag}{s}"))



def emit_l0a_a(p, ar, io):
    LN0 = 0
    ln_fm = ar.bf(LN0, [128, 16, T])
    emit_ln_fm(p, ar, io["xin"], 0, NT, ln_fm, 32768, "dL")
    p.barrier()
    specs = [("w_qkv", 0, AF.Silu, io["qf"], None),
             ("w_qkv", 2048, AF.Silu, io["kf"], None),
             ("w_qkv", 4096, AF.Silu, None, io["vt"])]
    emit_proj_nat_heads(p, ar, io, ln_fm, specs, "d")
    GW0, GE0 = 32768, 49152
    gw = [ar.bf(GW0 + k * 8192, [128, 16, 512]) for k in range(2)]
    geb = [ar.f32(GE0 + k * 4096, [128, 2048]) for k in range(2)]
    for oR in range(4):
        wslot = gw[oR % 2]
        p.wait("gpsimd", "dgt", p.val("dgt"))
        wt = stream_cast(p, ar, wcols(io["w_gate"], oR * 512, (oR + 1) * 512),
                         wslot, 16, 512)
        p.wait("tensor", "stgc", wt)
        for i in range(NT):
            bank = ar.ps[2 + i % 2]
            if p.val("dgs") >= 2:
                p.wait("tensor", "dgs", p.val("dgs") - 1)
            for kc in range(NKC):
                tt = p.do("tensor", lambda e, kc=kc, i=i, wslot=wslot, bank=bank:
                          e.matmul(bank[:, :], ln_fm[:, kc, i * 128:(i + 1) * 128],
                                   wslot[:, kc, :], start=(kc == 0),
                                   stop=(kc == NKC - 1)),
                          inc=("dgt" if kc == NKC - 1 else None))
            p.wait("scalar", "dgt", p.val("dgt"))
            p.wait("scalar", f"dgb{i % 2}", p.val(f"dgb{i % 2}"))
            gs = p.do("scalar", lambda e, i=i, oR=oR, bank=bank: e.activation(
                out=geb[i % 2].bitcast(BF16)[:, 0:2048]
                if False else geb[i % 2].bitcast(BF16)[:, 0:512],
                in_=bank[:, :], func=AF.Sigmoid), inc="dgs")
            p.wait("gpsimd", "dgs", gs)
            p.dma("gpsimd", io["gate"][i * 128:(i + 1) * 128,
                                       oR * 512:(oR + 1) * 512],
                  geb[i % 2].bitcast(BF16)[:, 0:512], f"dgb{i % 2}")
    p.wait("gpsimd", "dgb0", p.val("dgb0"))
    p.wait("gpsimd", "dgb1", p.val("dgb1"))



def emit_l0a_b(p, ar, io):
    nc = p.nc
    Q0, K0, V0, O0 = 0, 8192, 16384, 24576
    DD0 = 33280
    CC0 = 34560
    qb = ar.bf(Q0, [128, SFULL])
    kb = ar.bf(K0, [128, SFULL])
    vb = ar.bf(V0, [128, 32, 128])
    ob = ar.bf(O0, [128, 32, 128])
    ddT = ar.bf(DD0, [128, 2, 256])
    dmask = ar.f32(CC0, [128, 2, 256])
    dtrelu = ar.f32(CC0 + 2048, [128, 2, 256])
    arr2 = ar.sm(0, [128, 2])
    arrm = ar.sm(2, [128, 2])
    scol = ar.sm(4, [128, 8])
    nscol = ar.sm(12, [128, 8])
    qdec = ar.sm(20, [128, 2])
    kdec = ar.sm(22, [128, 2])
    bdcol = ar.sm(24, [128, 1])
    kvf = ar.sm(32, [128, 128])
    sro = ar.rows[0:1, 0:8]
    srbf = ar.rowsbf[0:1, 8:16]
    kvbf = ar.bf(CC0 + 6144, [128, 128])
    ktm = [ar.bf(CC0 + 6272 + k * 128, [128, 128]) for k in range(2)]
    oi = ar.sm(160, [128, 256])

    t0 = p.dma("sync", sro, brow(io["slope_u"]), "el", slow=True)
    p.do("gpsimd", lambda e: e.iota(arr2[:, :], [[128, 2]], base=1,
                                    channel_multiplier=1,
                                    allow_small_or_imprecise_dtypes=True),
         inc="eg")
    p.do("gpsimd", lambda e: e.iota(arrm[:, :], [[128, 2]], base=1 - 256,
                                    channel_multiplier=1,
                                    allow_small_or_imprecise_dtypes=True),
         inc="eg")
    for mp in range(2):
        p.do("gpsimd", lambda e, mp=mp: e.iota(
            dmask[:, mp, :], [[1, 256]], base=-128 * mp, channel_multiplier=-1,
            allow_small_or_imprecise_dtypes=True), inc="eg")
    gv = p.val("eg")
    p.wait("vector", "eg", gv)

    def mkconst(e):
        for mp in range(2):
            e.tensor_scalar(dtrelu[:, mp, :], dmask[:, mp, :], 0.0, None,
                            op0=ALU.max)
            e.drain()
            e.tensor_single_scalar(dmask[:, mp, :], dmask[:, mp, :], -0.5,
                                   op=ALU.is_gt)
            e.drain()
    p.on("vector", mkconst)
    cv = p.mark("vector", "ev0")
    onesf = ar.rows[0:1, 128:256]
    p.do("vector", lambda e: e.memset(onesf, 1.0), inc="ev0")
    cv2 = p.val("ev0")
    p.wait("tensor", "ev0", cv2)
    p.wait("tensor", "el", t0)
    ts_ = p.do("tensor", lambda e: e.matmul(
        ar.ps[4][:, 0:8], onesf, sro, start=True, stop=True), inc="et0")
    p.wait("vector", "et0", ts_)
    p.drain("vector")
    p.on("vector", lambda e: e.tensor_copy(scol[:, :], ar.ps[4][:, 0:8]))
    p.drain("vector")
    p.do("vector", lambda e: e.tensor_scalar(nscol[:, :], scol[:, :], -1.0,
                                             None, op0=ALU.mult), inc="ev0")
    const_ready = p.val("ev0")
    p.wait("scalar", "ev0", const_ready)

    for u in range(8):
        lq = p.dma("sync", qb[:, :], io["qf"][u], "eq")
        lk = p.dma("sync", kb[:, :], io["kf"][u], "ek")
        lv = p.dma("sync", vb[:, :, :],
                   io["vt"][u].rearrange("(a p) c -> p a c", p=128), "ev")
        def dec(e, u=u):
            e.activation(out=qdec[:, :], in_=arr2[:, :], func=AF.Exp,
                         scale=nscol[:, u:u + 1])
            e.activation(out=kdec[:, :], in_=arrm[:, :], func=AF.Exp,
                         scale=scol[:, u:u + 1])
            e.activation(out=bdcol[:, :], in_=nscol[:, u:u + 1], func=AF.Exp,
                         scale=256.0)
            e.drain()
        p.wait("scalar", "ekv", p.val("ekv"))
        p.on("scalar", dec)
        for mp in range(2):
            p.do("scalar", lambda e, mp=mp, u=u: e.activation(
                out=oi[:, 0:256] if False else ddT[:, mp, :],
                in_=dtrelu[:, mp, :], func=AF.Exp, scale=nscol[:, u:u + 1]),
                inc="es0")
        s0 = p.val("es0")
        p.wait("vector", "es0", s0)
        p.drain("vector")
        for mp in range(2):
            p.do("vector", lambda e, mp=mp: e.tensor_tensor(
                ddT[:, mp, :], ddT[:, mp, :], dmask[:, mp, :], op=ALU.mult),
                inc="ev1")
        dd_ready = p.val("ev1")
        p.do("vector", lambda e: e.memset(kvf[:, :], 0.0), inc="ev1")
        p.wait("tensor", "ev1", p.val("ev1"))
        p.wait("tensor", "eq", lq)
        p.wait("tensor", "ek", lk)
        p.wait("tensor", "ev", lv)

        for b in range(16):
            for mp in range(2):
                bank = ar.ps[mp]
                if p.val("eqkv") >= 2:
                    p.wait("tensor", "eqkv", p.val("eqkv") - 1)
                tt = p.do("tensor", lambda e, b=b, mp=mp, bank=bank: e.matmul(
                    bank[:, 0:256], kb[:, b * 256 + mp * 128: b * 256 + (mp + 1) * 128],
                    qb[:, b * 256:(b + 1) * 256], start=True, stop=True),
                    inc="eqkt")
                p.wait("vector", "eqkt", tt)
                if b == 0 and mp == 0:
                    p.wait("vector", "ev1", dd_ready)
                p.drain("vector")
                p.do("vector", lambda e, mp=mp, bank=bank: e.tensor_tensor(
                    ktm[mp][:, 0:128] if False else
                    _qkd_buf(ar)[mp][:, :], bank[:, 0:256], ddT[:, mp, :],
                    op=ALU.mult), inc="eqkv")
            qkd_ready = p.val("eqkv")
            p.wait("tensor", "eqkv", qkd_ready)
            for lp in range(2):
                bank = ar.ps[2 + lp]
                for mp in range(2):
                    tt = p.do("tensor", lambda e, b=b, lp=lp, mp=mp, bank=bank:
                              e.matmul(bank[:, 0:128],
                                       _qkd_buf(ar)[mp][:, lp * 128:(lp + 1) * 128],
                                       vb[:, b * 2 + mp, :],
                                       start=(mp == 0), stop=(mp == 1)),
                              inc=("eint" if mp == 1 else None))
            if b > 0:
                for lp in range(2):
                    bank = ar.ps[4 + lp]
                    tt = p.do("tensor", lambda e, b=b, lp=lp, bank=bank: e.matmul(
                        bank[:, 0:128],
                        qb[:, b * 256 + lp * 128: b * 256 + (lp + 1) * 128],
                        kvbf[:, :], start=True, stop=True),
                        inc=("eint" if lp == 1 else None))
            it = p.val("eint")
            p.wait("scalar", "eint", it)
            for lp in range(2):
                if b > 0:
                    p.do("scalar", lambda e, lp=lp: e.activation(
                        out=oi[:, lp * 128:(lp + 1) * 128], in_=ar.ps[4 + lp][:, 0:128],
                        func=AF.Identity, scale=qdec[:, lp:lp + 1]), inc="ees")
            es = p.val("ees")
            p.wait("vector", "ees", es)
            p.wait("vector", "eint", it)
            p.drain("vector")
            for lp in range(2):
                if b > 0:
                    p.do("vector", lambda e, b=b, lp=lp: e.tensor_tensor(
                        ob[:, b * 2 + lp, :], oi[:, lp * 128:(lp + 1) * 128],
                        ar.ps[2 + lp][:, 0:128], op=ALU.add), inc="eov")
                else:
                    p.do("vector", lambda e, b=b, lp=lp: e.tensor_copy(
                        ob[:, b * 2 + lp, :], ar.ps[2 + lp][:, 0:128]), inc="eov")
            for mp in range(2):
                tt = p.do("tensor", lambda e, b=b, mp=mp: e.transpose(
                    ar.pt[mp][:, 0:128],
                    kb[:, b * 256 + mp * 128: b * 256 + (mp + 1) * 128],
                    ar.ident[:, :]), inc="ekt")
                p.wait("scalar", "ekt", tt)
                p.wait("scalar", "ekv", p.val("ekv"))
                p.do("scalar", lambda e, mp=mp: e.activation(
                    out=ktm[mp][:, :], in_=ar.pt[mp][:, 0:128], func=AF.Identity,
                    scale=kdec[:, mp:mp + 1]), inc="eks")
            ks = p.val("eks")
            p.wait("tensor", "eks", ks)
            bank = ar.ps[0]
            p.wait("tensor", "eqkv", qkd_ready)
            for mp in range(2):
                tt = p.do("tensor", lambda e, b=b, mp=mp, bank=bank: e.matmul(
                    bank[:, 0:128], ktm[mp][:, :], vb[:, b * 2 + mp, :],
                    start=(mp == 0), stop=(mp == 1)), inc=("ekut" if mp == 1 else None))
            kt = p.val("ekut")
            p.wait("vector", "ekut", kt)
            p.drain("vector")
            p.on("vector", lambda e, bank=bank: e.scalar_tensor_tensor(
                kvf[:, :], kvf[:, :], bdcol[:, :], bank[:, 0:128],
                op0=ALU.mult, op1=ALU.add))
            p.drain("vector")
            kv2 = p.mark("vector", "ekv")
            p.wait("scalar", "ekv", kv2)
            sk = p.do("scalar", lambda e: e.activation(
                out=kvbf[:, :], in_=kvf[:, :], func=AF.Identity), inc="ekv")
            p.wait("tensor", "ekv", p.val("ekv"))
        ov = p.val("eov")
        p.wait("gpsimd", "eov", ov)
        st = p.dma("gpsimd", io["of"][u].rearrange("(a p) c -> p a c", p=128),
                   ob[:, :, :], "est")
        p.wait("sync", "est", st)
        p.wait("sync", "ekv", p.val("ekv"))
        p.wait("sync", "ekut", p.val("ekut"))
    p.wait("gpsimd", "est", p.val("est"))


def _qkd_buf(ar):
    return [ar.bf(40960 + k * 256, [128, 256]) for k in range(2)]



def emit_l0a_c(p, ar, io):
    EPS_RMS = 1e-6
    OT0, GT0, OF0, WS0, XB0 = 0, 8192, 16384, 49152, 57344
    osl = [ar.bf(OT0 + k * 2048, [128, 2048]) for k in range(2)]
    gsl = [ar.bf(GT0 + k * 2048, [128, 2048]) for k in range(2)]
    o_fm = ar.bf(OF0, [128, 16, T])
    wsl = [ar.bf(WS0 + k * 8192, [128, 16, 512]) for k in range(2)]
    XB0 = 65536
    xsl = [ar.f32(XB0 + k * 2048, [128, 1024]) for k in range(2)]
    ybf = [ar.f32(XB0 + 4096 + k * 2048, [128, 1024]) for k in range(2)]
    rmsc = ar.sm(0, [128, 16])
    stats = ar.sm(16, [128, 8])
    junk2 = ar.bf(73728, [128, 2048])
    t0 = p.dma("sync", rmsc[:, :], bcol(io["rms_weight"]), "gl", slow=True)
    p.wait("scalar", "gl", t0)

    for i in range(NT):
        opar = f"gol{i % 2}"
        if i >= 2:
            p.wait("sync", "gtt", p.val("gtt") - 2)
        lo = p.dma("sync", osl[i % 2][:, :], io["otm"][i * 128:(i + 1) * 128, :],
                   opar)
        lg = p.dma("sync", gsl[i % 2][:, :], io["gate"][i * 128:(i + 1) * 128, :],
                   opar)
        p.wait("scalar", opar, max(lo, lg))
        ss = p.do("scalar", lambda e, i=i: e.activation(
            out=junk2[:, :], in_=osl[i % 2][:, :], func=AF.Square,
            accum_out=stats[:, 0:1]), inc="gss")
        p.wait("vector", "gss", ss)
        p.drain("vector")

        def rstat(e):
            e.tensor_scalar(stats[:, 1:2], stats[:, 0:1], 1.0 / D, EPS_RMS,
                            op0=ALU.mult, op1=ALU.add)
            e.drain()
        p.on("vector", rstat)
        rv = p.mark("vector", "gvv0")
        p.wait("scalar", "gvv0", rv)
        s2 = p.do("scalar", lambda e: e.activation(
            out=stats[:, 1:2], in_=stats[:, 1:2], func=AF.Sqrt), inc="gss")
        p.wait("vector", "gss", s2)
        p.drain("vector")
        p.on("vector", lambda e: e.reciprocal(stats[:, 2:3], stats[:, 1:2]))
        p.drain("vector")
        rv2 = p.do("vector", lambda e, i=i: e.tensor_scalar(
            osl[i % 2][:, :], osl[i % 2][:, :], stats[:, 2:3], None,
            op0=ALU.mult), inc="gvv1")
        p.drain("vector")
        rv3 = p.do("vector", lambda e, i=i: e.tensor_tensor(
            osl[i % 2][:, :], osl[i % 2][:, :], gsl[i % 2][:, :], op=ALU.mult),
            inc="gvv")
        p.wait("tensor", "gvv", rv3)
        for half in range(2):
            bank = ar.pt[half]
            if i >= 1:
                p.wait("tensor", "gev", p.val("gev") - 1)
            for j8 in range(8):
                kc = half * 8 + j8
                tt = p.do("tensor", lambda e, i=i, kc=kc, j8=j8, bank=bank:
                          e.transpose(bank[:, j8 * 128:(j8 + 1) * 128],
                                      osl[i % 2][:, kc * 128:(kc + 1) * 128],
                                      ar.ident[:, :]),
                          inc=("gtt" if j8 == 7 else None))
            p.wait("scalar", "gtt", p.val("gtt"))
            for j8 in range(8):
                kc = half * 8 + j8
                p.do("scalar", lambda e, i=i, kc=kc, j8=j8, bank=bank:
                     e.activation(out=o_fm[:, kc, i * 128:(i + 1) * 128],
                                  in_=bank[:, j8 * 128:(j8 + 1) * 128],
                                  func=AF.Identity, scale=rmsc[:, kc:kc + 1]),
                     inc=("gev" if j8 == 7 else "gev8"))
    p.barrier()

    for oR in range(4):
        wslot = wsl[oR % 2]
        p.wait("gpsimd", "gmt", p.val("gmt"))
        wt = stream_cast(p, ar, wcols(io["w_out"], oR * 512, (oR + 1) * 512),
                         wslot, 16, 512)
        p.wait("tensor", "stgc", wt)
        for i in range(NT):
            bank = ar.ps[2 + i % 2]
            if p.val("gyv") >= 2:
                p.wait("tensor", "gyv", p.val("gyv") - 1)
            for kc in range(NKC):
                tt = p.do("tensor", lambda e, kc=kc, i=i, wslot=wslot, bank=bank:
                          e.matmul(bank[:, :], o_fm[:, kc, i * 128:(i + 1) * 128],
                                   wslot[:, kc, :], start=(kc == 0),
                                   stop=(kc == NKC - 1)),
                          inc=("gmt" if kc == NKC - 1 else None))
            s = i % 2
            xpar = f"gxl{s}"
            p.wait("sync", f"gys{s}", p.val(f"gys{s}"))
            xt = p.dma("sync", xsl[s][:, 0:512],
                       io["xin"][i * 128:(i + 1) * 128, oR * 512:(oR + 1) * 512],
                       xpar)
            p.wait("vector", xpar, xt)
            p.wait("vector", "gmt", p.val("gmt"))
            p.wait("vector", f"gys{s}", p.val(f"gys{s}"))
            p.drain("vector")
            yv = p.do("vector", lambda e, s=s, bank=bank: e.tensor_tensor(
                ybf[s][:, 0:512], xsl[s][:, 0:512], bank[:, :], op=ALU.add),
                inc="gyv")
            p.wait("gpsimd", "gyv", yv)
            p.dma("gpsimd", io["xout"][i * 128:(i + 1) * 128,
                                       oR * 512:(oR + 1) * 512],
                  ybf[s][:, 0:512], f"gys{s}")
    p.wait("gpsimd", "gys0", p.val("gys0"))
    p.wait("gpsimd", "gys1", p.val("gys1"))


def build_l0a_a():
    nc = bass.Bass("TRN2", target_bir_lowering=False)
    p = Prog(nc)
    ar = Arena(nc, p)
    io = {"xin": nc.dram_tensor("xin", [T, D], F32, kind="ExternalInput").ap(),
          "w_qkv": nc.dram_tensor("w_qkv", [D, 3 * D], F32, kind="ExternalInput").ap(),
          "w_gate": nc.dram_tensor("w_gate", [D, D], F32, kind="ExternalInput").ap(),
          "qf": nc.dram_tensor("qf", [16, 128, T], BF16, kind="ExternalOutput").ap(),
          "kf": nc.dram_tensor("kf", [16, 128, T], BF16, kind="ExternalOutput").ap(),
          "vt": nc.dram_tensor("vt", [16, T, 128], BF16, kind="ExternalOutput").ap(),
          "gate": nc.dram_tensor("gate", [T, D], BF16, kind="ExternalOutput").ap()}
    emit_l0a_a(p, ar, io)
    p.emit()
    return nc


def build_l0a_b():
    nc = bass.Bass("TRN2", target_bir_lowering=False)
    p = Prog(nc)
    ar = Arena(nc, p)
    io = {"qf": nc.dram_tensor("qf", [8, 128, SFULL], BF16, kind="ExternalInput").ap(),
          "kf": nc.dram_tensor("kf", [8, 128, SFULL], BF16, kind="ExternalInput").ap(),
          "vt": nc.dram_tensor("vt", [8, SFULL, 128], BF16, kind="ExternalInput").ap(),
          "slope_u": nc.dram_tensor("slope_u", [8], F32, kind="ExternalInput").ap(),
          "of": nc.dram_tensor("of", [8, SFULL, 128], BF16, kind="ExternalOutput").ap()}
    emit_l0a_b(p, ar, io)
    p.emit()
    return nc


def build_l0a_c():
    nc = bass.Bass("TRN2", target_bir_lowering=False)
    p = Prog(nc)
    ar = Arena(nc, p)
    io = {"xin": nc.dram_tensor("xin", [T, D], F32, kind="ExternalInput").ap(),
          "otm": nc.dram_tensor("otm", [T, D], BF16, kind="ExternalInput").ap(),
          "gate": nc.dram_tensor("gate", [T, D], BF16, kind="ExternalInput").ap(),
          "rms_weight": nc.dram_tensor("rms_weight", [D], F32, kind="ExternalInput").ap(),
          "w_out": nc.dram_tensor("w_out", [D, D], F32, kind="ExternalInput").ap(),
          "xout": nc.dram_tensor("xout", [T, D], F32, kind="ExternalOutput").ap()}
    emit_l0a_c(p, ar, io)
    p.emit()
    return nc



def emit_l1b_b(p, ar, io):
    nc = p.nc
    ISC = float(1.0 / np.sqrt(128.0))
    Q0, K0, V0, AM0, AD0, MS0 = 0, 8192, 16384, 24576, 28672, 33280
    qb = ar.bf(Q0, [128, SFULL])
    kb = ar.bf(K0, [128, SFULL])
    vb = ar.bf(V0, [128, 32, 128])
    amb = ar.bf(AM0, [128, 32, 128])
    adb = ar.bf(AD0, [128, 32, 128])
    msk = ar.f32(MS0, [128, 4, 512])
    sqf = ar.bf(MS0 + 4096, [128, 512])
    skf = ar.bf(MS0 + 4608, [128, 512])
    kkflat = ar.bf(MS0 + 5120, [128, 512])
    kktm = ar.bf(MS0 + 5120, [128, 4, 128])
    skflat = ar.bf(MS0 + 5632, [128, 512])
    sktm = ar.bf(MS0 + 5632, [128, 4, 128])
    ptb = ar.bf(MS0 + 6144, [128, 4, 128])
    dlt = ar.bf(MS0 + 6656, [128, 4, 128])
    membf = ar.bf(MS0 + 7168, [128, 128])
    zbf = ar.bf(MS0 + 7296, [128, 1])
    scr = ar.bf(MS0 + 7424, [128, 512])
    amtm = ar.bf(MS0 + 7936, [128, 128])
    adtm = ar.bf(MS0 + 8064, [128, 128])
    scr2 = ar.f32(MS0 + 8192, [128, 512])
    Ssb = ar.f32(MS0 + 9216, [128, 512])
    Pbf = ar.bf(MS0 + 10240, [128, 512])
    memf = ar.sm(0, [128, 128])
    zf = ar.sm(128, [128, 1])
    st_ = ar.sm(130, [128, 16])

    for midx in range(4):
        p.do("gpsimd", lambda e, midx=midx: e.iota(
            msk[:, midx, :], [[1, 512]], base=-midx * 128, channel_multiplier=-1,
            allow_small_or_imprecise_dtypes=True), inc="hmg")
    p.wait("vector", "hmg", p.val("hmg"))

    def mkmask(e):
        for midx in range(4):
            e.tensor_single_scalar(msk[:, midx, :], msk[:, midx, :], 0.5,
                                   op=ALU.is_gt)
            e.drain()
            e.tensor_scalar(msk[:, midx, :], msk[:, midx, :], -1e30, None,
                            op0=ALU.mult)
            e.drain()
    p.on("vector", mkmask)
    mv = p.mark("vector", "hmv")
    p.wait("scalar", "hmv", mv)
    p.wait("tensor", "hmv", mv)

    def elu1(tag, src_ap, dst_ap, n):
        p.drain("vector")
        ev = p.do("vector", lambda e: e.tensor_scalar(
            scr2[:, 0:n], src_ap, 0.0, None, op0=ALU.min), inc=tag)
        p.wait("scalar", tag, ev)
        p.wait("scalar", "hdv", p.val("hdv"))
        p.wait("scalar", "hpt", p.val("hpt"))
        sv = p.do("scalar", lambda e: e.activation(
            out=scr[:, 0:n], in_=scr2[:, 0:n], func=AF.Exp), inc=tag + "s")
        p.wait("vector", tag + "s", sv)
        p.drain("vector")
        p.on("vector", lambda e: e.tensor_scalar(
            scr2[:, 0:n], src_ap, 0.0, None, op0=ALU.max))
        p.drain("vector")
        return p.do("vector", lambda e: e.tensor_tensor(
            dst_ap, scr2[:, 0:n], scr[:, 0:n], op=ALU.add), inc=tag)

    for u in range(8):
        lq = p.dma("sync", qb[:, :], io["qf"][u], "hq")
        lk = p.dma("sync", kb[:, :], io["kf"][u], "hk")
        lv = p.dma("sync", vb[:, :, :],
                   io["vt"][u].rearrange("(a p) c -> p a c", p=128), "hv")
        p.wait("vector", "hst", p.val("hst"))
        p.drain("vector")
        p.on("vector", lambda e: e.memset(memf[:, :], 0.0))
        p.on("vector", lambda e: e.memset(zf[:, :], 1.0 / 128.0))
        p.drain("vector")
        iv = p.mark("vector", "hiv")
        p.wait("scalar", "hiv", iv)
        p.on("scalar", lambda e: e.activation(out=membf[:, :], in_=memf[:, :],
                                              func=AF.Identity))
        sv0 = p.do("scalar", lambda e: e.activation(out=zbf[:, :], in_=zf[:, :],
                                                    func=AF.Identity), inc="his")
        p.wait("tensor", "his", sv0)
        for eng in ("vector", "tensor"):
            p.wait(eng, "hq", lq)
            p.wait(eng, "hk", lk)
        p.wait("tensor", "hv", lv)

        for sg in range(8):
            t0c = sg * 512
            sq_done = elu1("hsq", qb[:, t0c:t0c + 512], sqf[:, :], 512)
            sk_done = elu1("hsk", kb[:, t0c:t0c + 512], skf[:, :], 512)
            p.wait("tensor", "hktc", p.val("hktc"))
            for t4 in range(4):
                if t4 >= 2:
                    p.wait("tensor", "hktc", p.val("hktc") - 1)
                tt = p.do("tensor", lambda e, t4=t4, t0c=t0c: e.transpose(
                    ar.pt[t4 % 2][:, 0:128],
                    kb[:, t0c + t4 * 128:t0c + (t4 + 1) * 128],
                    ar.ident[:, :]), inc="hkt")
                p.wait("vector", "hkt", tt)
                p.drain("vector")
                p.do("vector", lambda e, t4=t4: e.tensor_copy(
                    kktm[:, t4, :], ar.pt[t4 % 2][:, 0:128]), inc="hktc")
            skt_done = elu1("hskt", kkflat[:, :], skflat[:, :], 512)
            p.wait("tensor", "hsq", sq_done)
            p.wait("tensor", "hsk", sk_done)
            p.wait("tensor", "hskt", skt_done)
            p.wait("tensor", "hzv", p.val("hzv"))
            for sp in range(4):
                p.do("tensor", lambda e, sp=sp: e.matmul(
                    ar.ps[0][:, sp:sp + 1], sqf[:, sp * 128:(sp + 1) * 128],
                    zbf[:, :], start=True, stop=True))
            for sp in range(4):
                tt = p.do("tensor", lambda e, sp=sp: e.matmul(
                    ar.ps[0][:, 4 + sp:5 + sp], skf[:, sp * 128:(sp + 1) * 128],
                    zbf[:, :], start=True, stop=True),
                    inc=("hzt" if sp == 3 else None))
            p.wait("vector", "hzt", p.val("hzt"))
            p.drain("vector")
            p.on("vector", lambda e: e.tensor_copy(st_[:, 4:12], ar.ps[0][:, 0:8]))
            p.drain("vector")
            zr = p.do("vector", lambda e: e.reciprocal(st_[:, 4:12],
                                                       st_[:, 4:12]), inc="hzv")
            p.wait("scalar", "hzv", zr)
            p.wait("tensor", "hzv", zr)

            for sp in range(4):
                bank = ar.ps[2 + sp]
                tS = p.do("tensor", lambda e, sp=sp, t0c=t0c, bank=bank: e.matmul(
                    bank[:, :], qb[:, t0c + sp * 128:t0c + (sp + 1) * 128],
                    kb[:, t0c:t0c + 512], start=True, stop=True), inc="hSt")
                p.wait("vector", "hSt", tS)
                p.drain("vector")
                sv = p.do("vector", lambda e, sp=sp, bank=bank: e.tensor_tensor(
                    Ssb[:, :], bank[:, :], msk[:, sp, :], op=ALU.add), inc="hSv")
                p.drain("vector")

                def pstat(e):
                    e.reduce_max(st_[:, 0:1], Ssb[:, :], axis=AX.X)
                    e.drain()
                    e.tensor_scalar(st_[:, 1:2], st_[:, 0:1], -ISC, None,
                                    op0=ALU.mult)
                    e.drain()
                p.on("vector", pstat)
                pv = p.mark("vector", "hpv")
                p.wait("scalar", "hpv", pv)
                p.wait("scalar", "hptc", p.val("hptc"))
                ps_ = p.do("scalar", lambda e: e.activation(
                    out=Pbf[:, :], in_=Ssb[:, :], func=AF.Exp, bias=st_[:, 1:2],
                    scale=ISC, accum_out=st_[:, 2:3]), inc="hps")
                p.wait("vector", "hps", ps_)
                p.drain("vector")
                lr = p.do("vector", lambda e: e.reciprocal(st_[:, 3:4],
                                                           st_[:, 2:3]), inc="hlv")
                p.wait("scalar", "hlv", lr)
                pn = p.do("scalar", lambda e: e.activation(
                    out=scr[:, :], in_=Pbf[:, :], func=AF.Identity,
                    scale=st_[:, 3:4]), inc="hps2")
                p.wait("tensor", "hps2", pn)
                p.wait("tensor", "hptc", p.val("hptc"))
                for t4 in range(4):
                    if t4 >= 2:
                        p.wait("tensor", "hptc", p.val("hptc") - 1)
                    tt = p.do("tensor", lambda e, t4=t4: e.transpose(
                        ar.pt[t4 % 2][:, 0:128], scr[:, t4 * 128:(t4 + 1) * 128],
                        ar.ident[:, :]), inc="hpt")
                    p.wait("vector", "hpt", tt)
                    p.drain("vector")
                    p.do("vector", lambda e, t4=t4: e.tensor_copy(
                        ptb[:, t4, :], ar.pt[t4 % 2][:, 0:128]), inc="hptc")
                ptc = p.val("hptc")
                p.wait("tensor", "hptc", ptc)
                for t4 in range(4):
                    tt = p.do("tensor", lambda e, sp=sp, sg=sg, t4=t4, bank=bank:
                              e.matmul(bank[:, 0:128], ptb[:, t4, :],
                                       vb[:, sg * 4 + t4, :],
                                       start=(t4 == 0), stop=(t4 == 3)),
                              inc=("hadt" if t4 == 3 else None))
                p.wait("tensor", "hats", p.val("hats"))
                tm_ = p.do("tensor", lambda e, sp=sp, t0c=t0c: e.matmul(
                    ar.ps[1][:, 0:128], sqf[:, sp * 128:(sp + 1) * 128],
                    membf[:, :], start=True, stop=True), inc="hamt")
                p.wait("scalar", "hamt", p.val("hamt"))
                p.wait("scalar", "hadt", p.val("hadt"))
                p.wait("scalar", "hatv", p.val("hatv"))
                p.on("scalar", lambda e, sp=sp: e.activation(
                    out=amtm[:, :], in_=ar.ps[1][:, 0:128], func=AF.Identity,
                    scale=st_[:, 4 + sp:5 + sp]))
                sa = p.do("scalar", lambda e, bank=bank: e.activation(
                    out=adtm[:, :], in_=bank[:, 0:128], func=AF.Identity),
                    inc="hats")
                p.wait("tensor", "hats", sa)
                for mi, (src_, dstb) in enumerate(((amtm, amb), (adtm, adb))):
                    tt = p.do("tensor", lambda e, src_=src_: e.transpose(
                        ar.pt[mi % 2][:, 0:128] if False else ar.pt[0][:, 0:128],
                        src_[:, :], ar.ident[:, :]), inc="hatt")
                    p.wait("vector", "hatt", tt)
                    p.drain("vector")
                    p.do("vector", lambda e, sg=sg, sp=sp, dstb=dstb: e.tensor_copy(
                        dstb[:, sg * 4 + sp, :], ar.pt[0][:, 0:128]), inc="hatv")
                    p.wait("tensor", "hatv", p.val("hatv"))
            for t4 in range(4):
                p.wait("tensor", "hcs", p.val("hcs"))
                ct = p.do("tensor", lambda e, t4=t4: e.matmul(
                    ar.ps[1][:, 0:128], skf[:, t4 * 128:(t4 + 1) * 128],
                    membf[:, :], start=True, stop=True), inc="hct")
                p.wait("scalar", "hct", ct)
                p.wait("scalar", "hdv", p.val("hdv"))
                cs = p.do("scalar", lambda e, t4=t4: e.activation(
                    out=scr[:, 0:128], in_=ar.ps[1][:, 0:128], func=AF.Identity,
                    scale=st_[:, 8 + t4:9 + t4]), inc="hcs")
                p.wait("vector", "hcs", cs)
                p.drain("vector")
                p.do("vector", lambda e, sg=sg, t4=t4: e.tensor_tensor(
                    dlt[:, t4, :], vb[:, sg * 4 + t4, :], scr[:, 0:128],
                    op=ALU.subtract), inc="hdv")
            dv_ = p.val("hdv")
            p.wait("tensor", "hdv", dv_)
            for t4 in range(4):
                tt = p.do("tensor", lambda e, t4=t4: e.matmul(
                    ar.ps[0][:, 0:128], sktm[:, t4, :], dlt[:, t4, :],
                    start=(t4 == 0), stop=(t4 == 3)),
                    inc=("hmut" if t4 == 3 else None))
            mu_ = p.val("hmut")
            p.wait("vector", "hmut", mu_)
            p.drain("vector")
            p.on("vector", lambda e: e.tensor_tensor(
                memf[:, :], memf[:, :], ar.ps[0][:, 0:128], op=ALU.add))
            p.drain("vector")
            p.on("vector", lambda e: e.reduce_sum(st_[:, 12:13], skf[:, :],
                                                  axis=AX.X))
            p.drain("vector")
            zu = p.do("vector", lambda e: e.tensor_tensor(
                zf[:, :], zf[:, :], st_[:, 12:13], op=ALU.add), inc="hmv2")
            p.wait("scalar", "hmv2", zu)
            p.on("scalar", lambda e: e.activation(out=membf[:, :], in_=memf[:, :],
                                                  func=AF.Identity))
            s2 = p.do("scalar", lambda e: e.activation(out=zbf[:, :],
                                                       in_=zf[:, :],
                                                       func=AF.Identity),
                      inc="his")
            p.wait("tensor", "his", s2)
            p.wait("vector", "his", s2)
        av = p.val("hatv")
        p.wait("gpsimd", "hatv", av)
        p.dma("gpsimd", io["am"][u], amb[:, :, :].rearrange("p a b -> p (a b)"),
              "hst")
        st = p.dma("gpsimd", io["ad"][u], adb[:, :, :].rearrange("p a b -> p (a b)"),
                   "hst")
        p.wait("sync", "hst", st)
        p.wait("sync", "hdv", p.val("hdv"))
        p.wait("sync", "hmv2", p.val("hmv2"))
        p.wait("sync", "his", p.val("his"))
    p.wait("gpsimd", "hst", p.val("hst"))



def emit_l1b_d(p, ar, io):
    AM0, AD0, WS0, XB0 = 0, 32768, 65536 + 4096, 65536
    amf = ar.bf(AM0, [128, 16, T])
    adf = ar.bf(AD0, [128, 16, T])
    wmsl = [ar.bf(69632 + k * 512, [128, 512]) for k in range(4)]
    sigb = ar.sm(0, [128, 16])
    sig1 = ar.sm(16, [128, 16])
    bst = ar.sm(32, [128, 16])
    xsl = [ar.f32(71680 + k * 2048, [128, 1024]) for k in range(2)]
    ybf = [ar.sm(256 + k * 512, [128, 512]) for k in range(2)]
    la = p.dma("sync", amf[:, :, :], io["amf"][:, :, :].rearrange("a p c -> p a c"),
               "il")
    lb = p.dma("sync", adf[:, :, :], io["adf"][:, :, :].rearrange("a p c -> p a c"),
               "il")
    tb = p.dma("sync", bst[:, :],
               io["betas"][0, :, 0, :].rearrange("h c -> c h"), "il", slow=True)
    p.wait("scalar", "il", p.val("il"))
    p.on("scalar", lambda e: e.activation(out=sigb[:, :], in_=bst[:, :],
                                          func=AF.Sigmoid))
    sg = p.do("scalar", lambda e: e.activation(
        out=sig1[:, :], in_=bst[:, :], func=AF.Sigmoid, scale=ar.negcol[:, :]),
        inc="is0")
    p.wait("gpsimd", "is0", sg)

    for i in range(NT):
        for oH in range(4):
            s = oH % 2
            bank = ar.ps[2 + s]
            xpar = f"ixl{s}"
            p.wait("sync", f"iys{s}", p.val(f"iys{s}"))
            xt = p.dma("sync", xsl[s][:, 0:512],
                       io["xin"][i * 128:(i + 1) * 128,
                                 oH * 512:(oH + 1) * 512], xpar)
            for kc in range(NKC):
                slot = (kc % 2) * 2
                if p.val("ikc") >= 2:
                    p.wait("gpsimd", "ikc", p.val("ikc") - 1)
                wmt = stream_cast(p, ar, io["Wo"][kc * 128:(kc + 1) * 128,
                                                  oH * 512:(oH + 1) * 512],
                                  wmsl[slot][:, :], 1, 512)
                p.wait("gpsimd", "stgc", wmt)
                p.wait("gpsimd", "is0", sg)
                p.do("gpsimd", lambda e, slot=slot, kc=kc: e.tensor_scalar(
                    wmsl[slot + 1][:, :], wmsl[slot][:, :], sig1[:, kc:kc + 1],
                    None, op0=ALU.mult), inc="iwd")
                wdt_ = p.val("iwd")
                p.do("gpsimd", lambda e, slot=slot, kc=kc: e.tensor_scalar(
                    wmsl[slot][:, :], wmsl[slot][:, :], sigb[:, kc:kc + 1],
                    None, op0=ALU.mult), inc="iwd")
                wmt_ = p.val("iwd")
                p.wait("tensor", "iwd", wmt_)
                p.do("tensor", lambda e, i=i, kc=kc, slot=slot, bank=bank:
                     e.matmul(bank[:, :], amf[:, kc, i * 128:(i + 1) * 128],
                              wmsl[slot][:, :], start=(kc == 0), stop=False))
                tt = p.do("tensor", lambda e, i=i, kc=kc, slot=slot, bank=bank:
                          e.matmul(bank[:, :], adf[:, kc, i * 128:(i + 1) * 128],
                                   wmsl[slot + 1][:, :], start=False,
                                   stop=(kc == NKC - 1)), inc="ikc")
            mt = p.val("ikc")
            p.wait("vector", xpar, xt)
            p.wait("vector", "ikc", mt)
            p.wait("vector", f"iys{s}", p.val(f"iys{s}"))
            p.drain("vector")
            yv = p.do("vector", lambda e, s=s, bank=bank: e.tensor_tensor(
                ybf[s][:, :], xsl[s][:, 0:512], bank[:, :], op=ALU.add),
                inc="iyv")
            p.wait("gpsimd", "iyv", yv)
            p.dma("gpsimd", io["xout"][i * 128:(i + 1) * 128,
                                       oH * 512:(oH + 1) * 512],
                  ybf[s][:, :], f"iys{s}")
    p.wait("gpsimd", "iys0", p.val("iys0"))
    p.wait("gpsimd", "iys1", p.val("iys1"))


L1B_WSHAPES = {"Wq": [D, D], "Wk": [D, D], "Wv": [D, D], "Wo": [D, D],
               "betas": [1, 16, 1, 128]}


def emit_l1b_a(p, ar, io):
    ln_fm = ar.bf(0, [128, 16, T])
    emit_ln_fm(p, ar, io["xin"], 0, NT, ln_fm, 32768, "iL")
    p.barrier()
    specs = [("Wq", 0, AF.Identity, io["qf"], None),
             ("Wk", 0, AF.Identity, io["kf"], None),
             ("Wv", 0, AF.Identity, None, io["vt"])]
    emit_proj_nat_heads(p, ar, io, ln_fm, specs, "i")


def build_l1b_a():
    nc = bass.Bass("TRN2", target_bir_lowering=False)
    p = Prog(nc)
    ar = Arena(nc, p)
    io = {"xin": nc.dram_tensor("xin", [T, D], F32, kind="ExternalInput").ap(),
          "Wq": nc.dram_tensor("Wq", [D, D], F32, kind="ExternalInput").ap(),
          "Wk": nc.dram_tensor("Wk", [D, D], F32, kind="ExternalInput").ap(),
          "Wv": nc.dram_tensor("Wv", [D, D], F32, kind="ExternalInput").ap(),
          "qf": nc.dram_tensor("qf", [16, 128, T], BF16, kind="ExternalOutput").ap(),
          "kf": nc.dram_tensor("kf", [16, 128, T], BF16, kind="ExternalOutput").ap(),
          "vt": nc.dram_tensor("vt", [16, T, 128], BF16, kind="ExternalOutput").ap()}
    emit_l1b_a(p, ar, io)
    p.emit()
    return nc


def build_l1b_b():
    nc = bass.Bass("TRN2", target_bir_lowering=False)
    p = Prog(nc)
    ar = Arena(nc, p)
    io = {"qf": nc.dram_tensor("qf", [8, 128, SFULL], BF16, kind="ExternalInput").ap(),
          "kf": nc.dram_tensor("kf", [8, 128, SFULL], BF16, kind="ExternalInput").ap(),
          "vt": nc.dram_tensor("vt", [8, SFULL, 128], BF16, kind="ExternalInput").ap(),
          "am": nc.dram_tensor("am", [8, 128, SFULL], BF16, kind="ExternalOutput").ap(),
          "ad": nc.dram_tensor("ad", [8, 128, SFULL], BF16, kind="ExternalOutput").ap()}
    emit_l1b_b(p, ar, io)
    p.emit()
    return nc


def build_l1b_d():
    nc = bass.Bass("TRN2", target_bir_lowering=False)
    p = Prog(nc)
    ar = Arena(nc, p)
    io = {"xin": nc.dram_tensor("xin", [T, D], F32, kind="ExternalInput").ap(),
          "amf": nc.dram_tensor("amf", [16, 128, T], BF16, kind="ExternalInput").ap(),
          "adf": nc.dram_tensor("adf", [16, 128, T], BF16, kind="ExternalInput").ap(),
          "Wo": nc.dram_tensor("Wo", [D, D], F32, kind="ExternalInput").ap(),
          "betas": nc.dram_tensor("betas", [1, 16, 1, 128], F32,
                                  kind="ExternalInput").ap(),
          "xout": nc.dram_tensor("xout", [T, D], F32, kind="ExternalOutput").ap()}
    emit_l1b_d(p, ar, io)
    p.emit()
    return nc



_PROGS = {}


def _prog(name, builder):
    if name not in _PROGS:
        _PROGS[name] = builder()
    return _PROGS[name]


def _f32(a):
    return np.ascontiguousarray(np.asarray(a, dtype=np.float32))


def _i32(a):
    return np.ascontiguousarray(np.asarray(a, dtype=np.int32))


def _run(name, builder, in_maps):
    nc = _prog(name, builder)
    res = run_bass_kernel_spmd(nc, in_maps, core_ids=list(range(NCORE)))
    return res.results


def kernel(**inp):
    x = _f32(inp["L0_attn_a_hidden_states"]).reshape(16384, 2048)
    xs = [np.ascontiguousarray(x[c * T:(c + 1) * T]) for c in range(NCORE)]
    slope = _f32(inp["L0_attn_a_slope_rate"]).reshape(16)

    gidx = _i32(inp["L1_ffn_gate_idx"])
    uidx = _i32(inp["L1_ffn_up_idx"])
    didx = _i32(inp["L1_ffn_down_idx"])
    lut = _f32(inp["L1_ffn_lut"])
    rg = _run("g0", build_ghost_gather,
              [{"gidx": np.ascontiguousarray(gidx[:, c * 256:(c + 1) * 256]),
                "uidx": np.ascontiguousarray(uidx[:, c * 256:(c + 1) * 256]),
                "didx": np.ascontiguousarray(didx[c * 256:(c + 1) * 256, :]),
                "lut": lut} for c in range(NCORE)])
    wgt = np.concatenate([rg[c]["wgt"] for c in range(NCORE)], axis=0)
    wut = np.concatenate([rg[c]["wut"] for c in range(NCORE)], axis=0)
    wdt = np.concatenate([rg[c]["wdt"] for c in range(NCORE)], axis=2)

    def reshard_units(qf, kf, vt):
        qfu, kfu, vtu, slu = [], [], [], []
        for c in range(NCORE):
            b = c // 2
            heads = range(8 * (c % 2), 8 * (c % 2) + 8)
            qfu.append(np.stack([np.concatenate(
                [qf[2 * b][h], qf[2 * b + 1][h]], axis=1) for h in heads]))
            kfu.append(np.stack([np.concatenate(
                [kf[2 * b][h], kf[2 * b + 1][h]], axis=1) for h in heads]))
            vtu.append(np.stack([np.concatenate(
                [vt[2 * b][h], vt[2 * b + 1][h]], axis=0) for h in heads]))
            slu.append(np.ascontiguousarray(slope[list(heads)]))
        return qfu, kfu, vtu, slu

    def unit_of(b, h):
        return (2 * b + (h // 8), h % 8)

    w_qkv = _f32(inp["L0_attn_a_w_qkv"])
    w_gate = _f32(inp["L0_attn_a_w_gate"])
    ra = _run("l0a_a", build_l0a_a,
              [{"xin": xs[c], "w_qkv": w_qkv, "w_gate": w_gate}
               for c in range(NCORE)])
    qf = [ra[c]["qf"] for c in range(NCORE)]
    kf = [ra[c]["kf"] for c in range(NCORE)]
    vt = [ra[c]["vt"] for c in range(NCORE)]
    gate = [ra[c]["gate"] for c in range(NCORE)]
    qfu, kfu, vtu, slu = reshard_units(qf, kf, vt)
    rb = _run("l0a_b", build_l0a_b,
              [{"qf": qfu[c], "kf": kfu[c], "vt": vtu[c], "slope_u": slu[c]}
               for c in range(NCORE)])
    x1s = []
    otms = []
    for c in range(NCORE):
        b = c // 2
        rows = slice((c % 2) * T, (c % 2) * T + T)
        otm = np.empty((T, D), dtype=rb[0]["of"].dtype)
        for h in range(16):
            uc, ui = unit_of(b, h)
            otm[:, h * 128:(h + 1) * 128] = rb[uc]["of"][ui][rows]
        otms.append(otm)
    rms_w = _f32(inp["L0_attn_a_rms_weight"])
    w_out = _f32(inp["L0_attn_a_w_out"])
    rc = _run("l0a_c", build_l0a_c,
              [{"xin": xs[c], "otm": otms[c], "gate": gate[c],
                "rms_weight": rms_w, "w_out": w_out} for c in range(NCORE)])
    x1 = [rc[c]["xout"] for c in range(NCORE)]

    moe_w = {n: _f32(inp["L0_ffn_" + n]) for n in MOE_WNAMES}

    def run_moe(xcur):
        rm = _run("moe", build_moe,
                  [dict(moe_w, xin=xcur[c]) for c in range(NCORE)])
        return [rm[c]["xout"] for c in range(NCORE)]

    x2 = run_moe(x1)

    l0b_w = {"Wu": _f32(inp["L0_attn_b_Wu"]), "bu": _f32(inp["L0_attn_b_bu"]),
             "Wv": _f32(inp["L0_attn_b_Wv"]), "bv": _f32(inp["L0_attn_b_bv"]),
             "Wg1": _f32(inp["L0_attn_b_Wg1"]), "bg1": _f32(inp["L0_attn_b_bg1"]),
             "Wg2": _f32(inp["L0_attn_b_Wg2"]), "bg2": _f32(inp["L0_attn_b_bg2"]),
             "conv_w": _f32(inp["L0_attn_b_conv_w"]),
             "conv_b": _f32(inp["L0_attn_b_conv_b"])}
    halos = []
    for c in range(NCORE):
        left = x2[c - 1][-1] if c % 2 == 1 else np.zeros(D, np.float32)
        right = x2[c + 1][0] if c % 2 == 0 else np.zeros(D, np.float32)
        halos.append(np.ascontiguousarray(np.stack([left, right])))
    rba = _run("l0b_a", build_l0b_a,
               [dict(l0b_w, xin=x2[c], halo=halos[c]) for c in range(NCORE)])
    zt = np.zeros((128, NKC), np.float32)
    rbb = _run("l0b_b", build_l0b_b,
               [{"ug": rba[c]["ug"], "vg": rba[c]["vg"], "base": rba[c]["base"],
                 "totals_in": (rba[c - 1]["totals"] if c % 2 == 1 else zt),
                 "Wu": l0b_w["Wu"]} for c in range(NCORE)])
    x3 = [rbb[c]["xout"] for c in range(NCORE)]

    x4 = run_moe(x3)

    l1a_w = {n: _f32(inp["L1_attn_a_" + n]) for n in
             ("Wq", "bq", "Wk", "bk", "Wv", "bv", "Wo", "bo")}
    ra1 = _run("l1a_a", build_l1a_a,
               [{"xin": x4[c], "Wq": l1a_w["Wq"], "bq": l1a_w["bq"],
                 "Wk": l1a_w["Wk"], "bk": l1a_w["bk"], "Wv": l1a_w["Wv"],
                 "bv": l1a_w["bv"]} for c in range(NCORE)])
    qfu, kfu, vtu, _ = reshard_units([ra1[c]["qf"] for c in range(NCORE)],
                                     [ra1[c]["kf"] for c in range(NCORE)],
                                     [ra1[c]["vt"] for c in range(NCORE)])
    rb1 = _run("l1a_b", build_l1a_b,
               [{"qf": qfu[c], "kf": kfu[c], "vt": vtu[c]}
                for c in range(NCORE)])
    ofms = []
    for c in range(NCORE):
        b = c // 2
        cols = slice((c % 2) * T, (c % 2) * T + T)
        ofm = np.empty((16, 128, T), dtype=rb1[0]["of"].dtype)
        for h in range(16):
            uc, ui = unit_of(b, h)
            ofm[h] = rb1[uc]["of"][ui][:, cols]
        ofms.append(ofm)
    rc1 = _run("l1a_c", build_l1a_c,
               [{"xin": x4[c], "ofm": ofms[c], "Wo": l1a_w["Wo"],
                 "bo": l1a_w["bo"]} for c in range(NCORE)])
    x5 = [rc1[c]["xout"] for c in range(NCORE)]

    def run_l1f(xcur):
        rf = _run("l1f", build_l1f,
                  [{"xin": xcur[c], "wgt": wgt, "wut": wut, "wdt": wdt}
                   for c in range(NCORE)])
        return [rf[c]["xout"] for c in range(NCORE)]

    x6 = run_l1f(x5)

    l1b_w = {n: _f32(inp["L1_attn_b_" + n]) for n in
             ("Wq", "Wk", "Wv", "Wo", "betas")}
    ra2 = _run("l1b_a", build_l1b_a,
               [{"xin": x6[c], "Wq": l1b_w["Wq"], "Wk": l1b_w["Wk"],
                 "Wv": l1b_w["Wv"]} for c in range(NCORE)])
    qfu, kfu, vtu, _ = reshard_units([ra2[c]["qf"] for c in range(NCORE)],
                                     [ra2[c]["kf"] for c in range(NCORE)],
                                     [ra2[c]["vt"] for c in range(NCORE)])
    rb2 = _run("l1b_b", build_l1b_b,
               [{"qf": qfu[c], "kf": kfu[c], "vt": vtu[c]}
                for c in range(NCORE)])
    amfs, adfs = [], []
    for c in range(NCORE):
        b = c // 2
        cols = slice((c % 2) * T, (c % 2) * T + T)
        amf = np.empty((16, 128, T), dtype=rb2[0]["am"].dtype)
        adf = np.empty((16, 128, T), dtype=rb2[0]["ad"].dtype)
        for h in range(16):
            uc, ui = unit_of(b, h)
            amf[h] = rb2[uc]["am"][ui][:, cols]
            adf[h] = rb2[uc]["ad"][ui][:, cols]
        amfs.append(amf)
        adfs.append(adf)
    rd2 = _run("l1b_d", build_l1b_d,
               [{"xin": x6[c], "amf": amfs[c], "adf": adfs[c],
                 "Wo": l1b_w["Wo"], "betas": l1b_w["betas"]}
                for c in range(NCORE)])
    x7 = [rd2[c]["xout"] for c in range(NCORE)]

    x8 = run_l1f(x7)
    out = np.concatenate(x8, axis=0).reshape(4, 4096, 2048).astype(np.float32)
    return out
